# version 08
# baseline (speedup 1.0000x reference)
import numpy as np
import ml_dtypes
from contextlib import ExitStack

import concourse.bass as bass
import concourse.mybir as mybir
from concourse.bass_utils import run_bass_kernel_spmd

F32 = mybir.dt.float32
BF16 = mybir.dt.bfloat16
AF = mybir.ActivationFunctionType
ALU = mybir.AluOpType
AX = mybir.AxisListType

ENGS = ("tensor", "vector", "scalar", "sync", "gpsimd")


def hadamard(n):
    H = np.array([[1.0]])
    while H.shape[0] < n:
        H = np.block([[H, H], [H, -H]])
    return H


class Sem:

    def __init__(self, handle, name):
        self.h = handle
        self.name = name
        self.n = 0


class Rec:

    def __init__(self):
        self.prog = {e: [] for e in ENGS}
        self.mark = {e: {} for e in ENGS}

    def wait(self, eng, sem, val, force=False):
        if val <= 0:
            return
        if not force and self.mark[eng].get(sem.name, -1) >= val:
            return
        if self.mark[eng].get(sem.name, -1) < val:
            self.mark[eng][sem.name] = val
        self.prog[eng].append(lambda e, s=sem.h, v=val: e.wait_ge(s, v))

    def op(self, eng, fn, inc=None, inc_by=1):
        if inc is None:
            self.prog[eng].append(fn)
            return None
        inc.n += inc_by
        v = inc.n
        self.prog[eng].append(lambda e, f=fn, s=inc.h, k=inc_by: f(e).then_inc(s, k))
        if inc_by == 1 and self.mark[eng].get(inc.name, -1) < v:
            self.mark[eng][inc.name] = v
        return v


def build_program(D=2048, HF=8192, T=512, NTOK=4096, SEQ=2048, HEADS=32, DK=64,
                  n_phases=7):
    DC = D // 128
    HC = HF // 128
    NB = T // 128
    NT = NTOK // T
    TPS = SEQ // T
    HPC = 128 // DK
    NHB = NB * HEADS

    nc = bass.Bass("TRN2", target_bir_lowering=False)

    x_d = nc.dram_tensor("x", [DC, 128, NTOK], F32, kind="ExternalInput").ap()
    wq_d = nc.dram_tensor("wq", [DC, 128, DC, 128], F32, kind="ExternalInput").ap()
    wk_d = nc.dram_tensor("wk", [DC, 128, DC, 128], F32, kind="ExternalInput").ap()
    wv_d = nc.dram_tensor("wv", [DC, 128, DC, 128], F32, kind="ExternalInput").ap()
    wo_d = nc.dram_tensor("wo", [DC, 128, DC, 128], F32, kind="ExternalInput").ap()
    w1_d = nc.dram_tensor("w1", [HC, 128, DC, 128], F32, kind="ExternalInput").ap()
    w3_d = nc.dram_tensor("w3", [HC, 128, DC, 128], F32, kind="ExternalInput").ap()
    w2_d = nc.dram_tensor("w2", [DC, 128, HC, 128], F32, kind="ExternalInput").ap()
    out_d = nc.dram_tensor("out", [DC, 128, NTOK], F32, kind="ExternalOutput").ap()

    sq_d = nc.dram_tensor("sq", [DC, 128, DC, 128], BF16, kind="Internal").ap()
    sk_d = nc.dram_tensor("sk", [DC, 128, DC, 128], BF16, kind="Internal").ap()
    sv_d = nc.dram_tensor("sv", [DC, 128, DC, 128], BF16, kind="Internal").ap()
    so_d = nc.dram_tensor("so", [DC, 128, DC, 128], BF16, kind="Internal").ap()
    s1_d = nc.dram_tensor("s1", [HC, 128, DC, 128], BF16, kind="Internal").ap()
    s3_d = nc.dram_tensor("s3", [HC, 128, DC, 128], BF16, kind="Internal").ap()
    s2_d = nc.dram_tensor("s2", [DC, 128, HC, 128], BF16, kind="Internal").ap()

    ident_np = np.eye(128, dtype=ml_dtypes.bfloat16)
    Hb_np = hadamard(128).astype(ml_dtypes.bfloat16)
    r = np.arange(128)[:, None]
    c = np.arange(256)[None, :]
    validG = (c >= r) & (c <= r + 128)
    mG_np = np.where(validG, 0.0, -1e9).astype(ml_dtypes.bfloat16)
    m0_np = np.where(validG & (c >= 128), 0.0, -1e9).astype(ml_dtypes.bfloat16)
    onesn_np = np.full((128, 1), -1.0 / D, dtype=np.float32)
    onesp_np = np.full((128, 1), 1.0 / D, dtype=ml_dtypes.bfloat16)
    ones1_np = np.ones((1, 128), dtype=np.float32)
    eps_np = np.full((1, 1), 1e-5, dtype=np.float32)

    ident_d = nc.inline_tensor(ident_np, name="identC").ap()
    Hb_d = nc.inline_tensor(Hb_np, name="HbC").ap()
    mG_d = nc.inline_tensor(mG_np, name="mGC").ap()
    m0_d = nc.inline_tensor(m0_np, name="m0C").ap()
    onesn_d = nc.inline_tensor(onesn_np, name="onesnC").ap()
    onesp_d = nc.inline_tensor(onesp_np, name="onespC").ap()
    ones1_d = nc.inline_tensor(ones1_np, name="ones1C").ap()
    eps_d = nc.inline_tensor(eps_np, name="epsC").ap()

    R = Rec()

    with ExitStack() as ctx:
        E = ctx.enter_context

        x_sb = E(nc.sbuf_tensor("x_sb", [128, DC * T], F32))
        xn_sb = E(nc.sbuf_tensor("xn_sb", [128, DC * T], BF16))
        k_prev = E(nc.sbuf_tensor("k_prev", [128, DC * 128], BF16))
        v_prev = E(nc.sbuf_tensor("v_prev", [128, D], BF16))
        ident = E(nc.sbuf_tensor("ident", [128, 128], BF16))
        Hb = E(nc.sbuf_tensor("Hb", [128, 128], BF16))
        mm_bank_free = {}
        mG = E(nc.sbuf_tensor("mG", [128, 256], BF16))
        m0 = E(nc.sbuf_tensor("m0", [128, 256], BF16))
        onesn = E(nc.sbuf_tensor("onesn", [128, 1], F32))
        onesp = E(nc.sbuf_tensor("onesp", [128, 1], BF16))
        ones1 = E(nc.sbuf_tensor("ones1", [1, 128], F32))
        eps_sb = E(nc.sbuf_tensor("eps_sb", [1, 1], F32))
        meanrow = E(nc.sbuf_tensor("meanrow", [1, T], F32))
        m2row = E(nc.sbuf_tensor("m2row", [1, T], F32))
        varrow = E(nc.sbuf_tensor("varrow", [1, T], F32))
        lnrow = E(nc.sbuf_tensor("lnrow", [1, T], F32))
        rstdrow = E(nc.sbuf_tensor("rstdrow", [1, T], F32))
        sqb = [E(nc.sbuf_tensor(f"sqb{i}", [128, T], BF16)) for i in range(2)]
        xctmp = E(nc.sbuf_tensor("xctmp", [128, T], F32))
        nmx = [E(nc.sbuf_tensor(f"nmx{i}", [128, 1], F32)) for i in range(3)]
        smr = [E(nc.sbuf_tensor(f"smr{i}", [128, 1], F32)) for i in range(3)]
        rsr = [E(nc.sbuf_tensor(f"rsr{i}", [128, 1], F32)) for i in range(3)]
        p_sb = [E(nc.sbuf_tensor(f"p_sb{i}", [128, 256], F32)) for i in range(3)]
        pb_sb = [E(nc.sbuf_tensor(f"pb_sb{i}", [128, 256], BF16)) for i in range(2)]
        ptT = [E(nc.sbuf_tensor(f"ptT{i}", [128, 256], BF16)) for i in range(2)]
        sg = [E(nc.sbuf_tensor(f"sg{i}", [128, T], F32)) for i in range(2)]

        sems = {}
        for name in ("s_pe", "s_dve", "s_act", "s_pool", "s_wst", "s_wld",
                     "s_x", "s_st", "s_cvt"):
            sems[name] = Sem(E(nc.semaphore(name)), name)
        s_pe, s_dve, s_act, s_pool = (sems[k] for k in ("s_pe", "s_dve", "s_act", "s_pool"))
        s_wst, s_wld, s_x, s_st, s_cvt = (
            sems[k] for k in ("s_wst", "s_wld", "s_x", "s_st", "s_cvt"))
        NRING = 6
        s_rg = []
        for i in range(NRING):
            s_rg.append(Sem(E(nc.semaphore(f"s_rg{i}")), f"s_rg{i}"))
            sems[f"s_rg{i}"] = s_rg[i]
        s_w2b = []
        for i in range(2):
            s_w2b.append(Sem(E(nc.semaphore(f"s_w2b{i}")), f"s_w2b{i}"))
            sems[f"s_w2b{i}"] = s_w2b[i]
        s_xc = []
        s_sc = []
        for i in range(DC):
            s_xc.append(Sem(E(nc.semaphore(f"s_xc{i}")), f"s_xc{i}"))
            sems[f"s_xc{i}"] = s_xc[i]
            s_sc.append(Sem(E(nc.semaphore(f"s_sc{i}")), f"s_sc{i}"))
            sems[f"s_sc{i}"] = s_sc[i]

        s_cst = Sem(E(nc.semaphore("s_cst")), "s_cst")
        sems["s_cst"] = s_cst
        s_cw = {}
        for w in ("q", "k", "v", "o", "1", "3"):
            s_cw[w] = Sem(E(nc.semaphore(f"s_cw{w}")), f"s_cw{w}")
            sems[f"s_cw{w}"] = s_cw[w]
        s_wl3 = []
        for i in range(3):
            s_wl3.append(Sem(E(nc.semaphore(f"s_wl3{i}")), f"s_wl3{i}"))
            sems[f"s_wl3{i}"] = s_wl3[i]

        for dst, src in ((ident, ident_d), (Hb, Hb_d), (mG, mG_d), (m0, m0_d),
                         (onesn, onesn_d), (onesp, onesp_d), (ones1, ones1_d),
                         (eps_sb, eps_d)):
            R.op("sync", lambda e, d=dst, s=src: e.dma_start(d[:, :], s), inc=s_cst, inc_by=16)
        cst_all = s_cst.n

        cast_specs = {"q": (wq_d, sq_d, DC), "k": (wk_d, sk_d, DC),
                      "v": (wv_d, sv_d, DC), "o": (wo_d, so_d, DC),
                      "1": (w1_d, s1_d, HC), "3": (w3_d, s3_d, HC)}
        cast_done = {}

        def emit_casts(name):
            src, dst, n = cast_specs[name]
            for i in range(n):
                R.op("gpsimd", lambda e, s=src, d=dst, i=i: e.dma_start(
                    d[i].rearrange("p a b -> p (a b)"), s[i].rearrange("p a b -> p (a b)")),
                    inc=s_cw[name], inc_by=16)
            cast_done[name] = s_cw[name].n

        with nc.sbuf_tensor("slabX0", [128, HC * 128], BF16) as X0, \
             nc.sbuf_tensor("slabX1", [128, HC * 128], BF16) as X1, \
             nc.sbuf_tensor("slabX2", [128, HC * 128], BF16) as X2, \
             nc.sbuf_tensor("slabS", [128, HC * 128], BF16) as Ssl, \
             nc.sbuf_tensor("outsl0", [128, HC * 128], BF16) as O0, \
             nc.sbuf_tensor("outsl1", [128, HC * 128], BF16) as O1, \
             nc.psum_tensor("pW0", [128, 512], F32) as pW0, \
             nc.psum_tensor("pW1", [128, 512], F32) as pW1:
            X = [X0, X1, X2]
            OSL = [O0, O1]
            pW = [pW0, pW1]
            n_stage = int(np.log2(HC))
            mm_last = {}
            evac_last = {}
            store_done = {}

            def emit_slab_load(sl):
                if sl >= DC:
                    return
                if sl >= 3:
                    R.wait("gpsimd", s_pe, mm_last[sl - 3])
                R.op("gpsimd", lambda e, sl=sl: e.dma_start(
                    X[sl % 3][:, :], w2_d[sl].rearrange("p a b -> p (a b)")),
                    inc=s_wl3[sl % 3], inc_by=16)

            evac_g = {}
            for sl in range(min(3, DC)):
                emit_slab_load(sl)
            emit_casts("q")
            emit_casts("k")
            cast_after = {0: "v", 2: "o"}

            fin_was_S = False
            for sl in range(DC):
                R.wait("vector", s_wl3[sl % 3], 16 * (sl // 3 + 1))
                if fin_was_S and sl > 0:
                    R.wait("vector", s_pe, mm_last[sl - 1])
                src, dst = X[sl % 3], Ssl
                vlast = None
                for s in range(n_stage):
                    span = (2 ** s) * 128
                    a_s = src[:, :].rearrange("p (g two x) -> p g two x", two=2, x=span)
                    a_d = dst[:, :].rearrange("p (g two x) -> p g two x", two=2, x=span)
                    if vlast is not None:
                        R.wait("vector", s_dve, vlast, force=True)
                    R.op("vector", lambda e, a_d=a_d, a_s=a_s: e.tensor_tensor(
                        a_d[:, :, 0, :], a_s[:, :, 0, :], a_s[:, :, 1, :], ALU.add),
                        inc=s_dve)
                    vlast = R.op("vector", lambda e, a_d=a_d, a_s=a_s: e.tensor_tensor(
                        a_d[:, :, 1, :], a_s[:, :, 0, :], a_s[:, :, 1, :], ALU.subtract),
                        inc=s_dve)
                    src, dst = dst, src
                bfly_done = vlast
                fin = src
                fin_was_S = fin is Ssl
                R.wait("tensor", s_dve, bfly_done)
                if sl == 0:
                    R.wait("tensor", s_cst, cst_all)
                for g in range(HC // 4):
                    b = g % 2
                    if sl * (HC // 4) + g >= 2:
                        R.wait("tensor", s_act, evac_g[(sl * (HC // 4) + g) - 2])
                    for cc in range(4):
                        cch = g * 4 + cc
                        mm_last[sl] = R.op("tensor", lambda e, b=b, fin=fin, cch=cch, cc=cc: e.matmul(
                            pW[b][:, cc * 128:(cc + 1) * 128], Hb[:, :],
                            fin[:, cch * 128:(cch + 1) * 128],
                            start=(cc == 0), stop=(cc == 3)), inc=s_pe)
                    if g == 0 and sl >= 2:
                        R.wait("scalar", s_wst, store_done[sl - 2])
                    R.wait("scalar", s_pe, mm_last[sl])
                    evac_g[sl * (HC // 4) + g] = R.op(
                        "scalar", lambda e, b=b, g=g, sl=sl: e.activation(
                            OSL[sl % 2][:, g * 512:(g + 1) * 512], pW[b][:, :],
                            AF.Copy, scale=float(1.0 / np.sqrt(HF))), inc=s_act)
                evac_last[sl] = evac_g[sl * (HC // 4) + (HC // 4) - 1]
                R.wait("scalar", s_act, evac_last[sl], force=True)
                store_done[sl] = R.op("scalar", lambda e, sl=sl: e.dma_start(
                    s2_d[sl].rearrange("p a b -> p (a b)"), OSL[sl % 2][:, :]),
                    inc=s_wst, inc_by=16)
                emit_slab_load(sl + 3)
                if sl in cast_after:
                    emit_casts(cast_after[sl])

            for w in ("q", "k", "v", "o", "1", "3"):
                if w not in cast_done:
                    emit_casts(w)

            transform_stores_all = s_wst.n
            mm_bank_free[0] = ("s_act", evac_last[DC - 1])
            if DC >= 2:
                mm_bank_free[1] = ("s_act", evac_last[DC - 2])

        alias = E(nc.sbuf_tensor("alias", [128, 4 * DC * T], BF16))
        q_sb = alias[:, 0 * DC * T:1 * DC * T]
        k_cur = alias[:, 1 * DC * T:2 * DC * T]
        af_sb = alias[:, 2 * DC * T:3 * DC * T]
        v_cur = alias[:, 3 * DC * T:4 * DC * T]
        hidden = alias[:, :]
        assert HC * T == 4 * DC * T
        ring = [E(nc.sbuf_tensor(f"ring{i}", [128, DC * 128], BF16)) for i in range(NRING)]
        w2buf = [E(nc.sbuf_tensor(f"w2buf{i}", [128, HC * 128], BF16)) for i in range(2)]

        pF = [E(nc.psum_tensor(f"pF{i}", [128, T], F32)) for i in range(6)]
        pT = [E(nc.psum_tensor(f"pT{i}", [128, 256], BF16)) for i in range(2)]


        ring_i = [0]
        ring_consumed = {}

        def stream_stripe(src_ap):
            seq = ring_i[0]
            ring_i[0] += 1
            slot = ring[seq % NRING]
            sem = s_rg[seq % NRING]
            if seq - NRING in ring_consumed:
                R.wait("sync", s_pe, ring_consumed[seq - NRING])
            v = R.op("sync", lambda e, slot=slot, s=src_ap: e.dma_start(
                slot[:, :].rearrange("p (a b) -> p a b", b=128), s), inc=sem, inc_by=16)
            return seq, slot, (sem, v)

        def mark_consumed(seq, pe_val):
            ring_consumed[seq] = pe_val

        store_v = {}

        def bank_wait(eng, bank):
            if bank in mm_bank_free:
                sname, v = mm_bank_free[bank]
                R.wait(eng, sems[sname], v)

        last_xn = [0]
        carry_pool = [0, 0]
        w2_pe_prev = {0: None, 1: None}

        for tile in range(NT):
            t0 = tile * T
            seq_start = (tile % TPS) == 0

            if tile == 0:
                for e in ("tensor", "scalar", "vector"):
                    R.wait(e, s_cst, cst_all)
                for dc2 in range(DC):
                    R.op("sync", lambda e, dc2=dc2, t0=t0: e.dma_start(
                        x_sb[:, dc2 * T:(dc2 + 1) * T], x_d[dc2][:, t0:t0 + T]),
                        inc=s_x, inc_by=16)
            else:
                for dc2 in range(DC):
                    R.wait("gpsimd", s_sc[dc2], 16 * tile)
                    R.op("gpsimd", lambda e, dc2=dc2, t0=t0: e.dma_start(
                        x_sb[:, dc2 * T:(dc2 + 1) * T], x_d[dc2][:, t0:t0 + T]),
                        inc=s_xc[dc2], inc_by=16)
            x_chunk = 16 * tile

            if n_phases <= 0:
                for dc2 in range(DC):
                    if tile == 0:
                        R.wait("gpsimd", s_x, 16 * DC)
                    else:
                        R.wait("gpsimd", s_xc[dc2], x_chunk)
                    R.op("gpsimd", lambda e, dc2=dc2, t0=t0: e.dma_start(
                        out_d[dc2][:, t0:t0 + T], x_sb[:, dc2 * T:(dc2 + 1) * T]),
                        inc=s_sc[dc2], inc_by=16)
                continue

            if seq_start:
                R.wait("gpsimd", s_pe, s_pe.n)
                carry_pool[0] = R.op("gpsimd", lambda e: e.memset(k_prev[:, :], 0.0), inc=s_pool)
                carry_pool[1] = R.op("gpsimd", lambda e: e.memset(v_prev[:, :], 0.0), inc=s_pool)

            def dbg_stores0(t0=t0, tile=tile, x_chunk=x_chunk):
                for sname in ("s_pe", "s_dve", "s_act", "s_pool"):
                    R.wait("gpsimd", sems[sname], sems[sname].n)
                for dc2 in range(DC):
                    if tile == 0:
                        R.wait("gpsimd", s_x, 16 * DC)
                    else:
                        R.wait("gpsimd", s_xc[dc2], x_chunk)
                    R.op("gpsimd", lambda e, dc2=dc2, t0=t0: e.dma_start(
                        out_d[dc2][:, t0:t0 + T], x_sb[:, dc2 * T:(dc2 + 1) * T]),
                        inc=s_sc[dc2], inc_by=16)

            if n_phases == 11:
                dbg_stores0()
                continue
            bank_wait("tensor", 0)
            bank_wait("tensor", 1)
            bank_wait("tensor", 2)
            bank_wait("tensor", 3)
            for dc2 in range(DC):
                if tile == 0:
                    R.wait("tensor", s_x, 16 * DC)
                else:
                    R.wait("tensor", s_xc[dc2], x_chunk)
                pe_mean = R.op("tensor", lambda e, dc2=dc2: e.matmul(
                    pF[0][0:1, :], onesn[:, :], x_sb[:, dc2 * T:(dc2 + 1) * T],
                    start=(dc2 == 0), stop=(dc2 == DC - 1)), inc=s_pe)
            sq_pe = {}
            for dc2 in range(DC):
                if tile == 0:
                    R.wait("scalar", s_x, 16 * DC)
                else:
                    R.wait("scalar", s_xc[dc2], x_chunk)
                if dc2 >= 2:
                    R.wait("scalar", s_pe, sq_pe[dc2 - 2])
                aqv = R.op("scalar", lambda e, dc2=dc2: e.activation(
                    sqb[dc2 % 2][:, :], x_sb[:, dc2 * T:(dc2 + 1) * T], AF.Square), inc=s_act)
                R.wait("tensor", s_act, aqv)
                sq_pe[dc2] = R.op("tensor", lambda e, dc2=dc2: e.matmul(
                    pF[1][0:1, :], onesp[:, :], sqb[dc2 % 2][:, :],
                    start=(dc2 == 0), stop=(dc2 == DC - 1)), inc=s_pe)
            if n_phases == 12:
                dbg_stores0()
                continue
            R.wait("vector", s_pe, pe_mean)
            v_mr = R.op("vector", lambda e: e.tensor_copy(meanrow[:, :], pF[0][0:1, :]), inc=s_dve)
            R.wait("scalar", s_dve, v_mr)
            a_m2 = R.op("scalar", lambda e: e.activation(m2row[:, :], meanrow[:, :], AF.Square),
                        inc=s_act)
            R.wait("vector", s_pe, sq_pe[DC - 1])
            R.wait("vector", s_act, a_m2)
            v_var = R.op("vector", lambda e: e.tensor_tensor(
                varrow[:, :], pF[1][0:1, :], m2row[:, :], ALU.subtract), inc=s_dve)
            R.wait("scalar", s_dve, v_var)
            a_ln = R.op("scalar", lambda e: e.activation(
                lnrow[:, :], varrow[:, :], AF.Ln, bias=eps_sb[0:1, 0:1]), inc=s_act)
            R.wait("scalar", s_act, a_ln, force=True)
            a_rstd = R.op("scalar", lambda e: e.activation(
                rstdrow[:, :], lnrow[:, :], AF.Exp, scale=-0.5), inc=s_act)
            if n_phases == 13:
                dbg_stores0()
                continue
            R.wait("tensor", s_dve, v_mr)
            pe_bc1 = R.op("tensor", lambda e: e.matmul(
                pF[2][:, :], ones1[:, :], meanrow[:, :], start=True, stop=True), inc=s_pe)
            R.wait("tensor", s_act, a_rstd)
            pe_bc2 = R.op("tensor", lambda e: e.matmul(
                pF[3][:, :], ones1[:, :], rstdrow[:, :], start=True, stop=True), inc=s_pe)
            if n_phases == 14:
                dbg_stores0()
                continue
            R.wait("vector", s_pe, pe_bc2)
            xn_done = {}
            for dc2 in range(DC):
                if dc2 > 0:
                    R.wait("vector", s_dve, last_xn[0], force=True)
                vadd = R.op("vector", lambda e, dc2=dc2: e.tensor_tensor(
                    xctmp[:, :], x_sb[:, dc2 * T:(dc2 + 1) * T], pF[2][:, :], ALU.add),
                    inc=s_dve)
                R.wait("vector", s_dve, vadd, force=True)
                last_xn[0] = R.op("vector", lambda e, dc2=dc2: e.tensor_tensor(
                    xn_sb[:, dc2 * T:(dc2 + 1) * T], xctmp[:, :], pF[3][:, :], ALU.mult),
                    inc=s_dve)
                xn_done[dc2] = last_xn[0]
            mm_bank_free[0] = ("s_dve", v_mr)
            mm_bank_free[1] = ("s_dve", v_var)
            mm_bank_free[2] = ("s_dve", last_xn[0])
            mm_bank_free[3] = ("s_dve", last_xn[0])

            def dbg_stores(t0=t0, tile=tile):
                for sname in ("s_pe", "s_dve", "s_act", "s_pool"):
                    R.wait("gpsimd", sems[sname], sems[sname].n)
                for dc2 in range(DC):
                    R.op("gpsimd", lambda e, dc2=dc2, t0=t0: e.dma_start(
                        out_d[dc2][:, t0:t0 + T], x_sb[:, dc2 * T:(dc2 + 1) * T]),
                        inc=s_sc[dc2], inc_by=16)

            if n_phases <= 1:
                dbg_stores()
                continue

            def proj(scr, evac):
                ready = []
                for oc in range(DC):
                    seq, slot, wv_ = stream_stripe(scr[oc])
                    b = 4 + (oc % 2)
                    R.wait("tensor", wv_[0], wv_[1])
                    bank_wait("tensor", b)
                    for dc2 in range(DC):
                        R.wait("tensor", s_dve, xn_done[dc2])
                        pev = R.op("tensor", lambda e, b=b, slot=slot, dc2=dc2: e.matmul(
                            pF[b][:, :], slot[:, dc2 * 128:(dc2 + 1) * 128],
                            xn_sb[:, dc2 * T:(dc2 + 1) * T],
                            start=(dc2 == 0), stop=(dc2 == DC - 1)), inc=s_pe)
                    mark_consumed(seq, pev)
                    sname, val = evac(oc, b, pev)
                    mm_bank_free[b] = (sname, val)
                    ready.append((sname, val))
                return ready

            def evac_q(oc, b, pev):
                R.wait("scalar", s_pe, pev)
                v = R.op("scalar", lambda e, oc=oc, b=b: e.activation(
                    q_sb[:, oc * T:(oc + 1) * T], pF[b][:, :], AF.Copy, scale=0.125),
                    inc=s_act)
                return ("s_act", v)

            def evac_k(oc, b, pev):
                R.wait("vector", s_pe, pev)
                v = R.op("vector", lambda e, oc=oc, b=b: e.tensor_copy(
                    k_cur[:, oc * T:(oc + 1) * T], pF[b][:, :]), inc=s_dve)
                return ("s_dve", v)

            def evac_v(oc, b, pev):
                R.wait("vector", s_pe, pev)
                v = R.op("vector", lambda e, oc=oc, b=b: e.tensor_copy(
                    af_sb[:, oc * T:(oc + 1) * T], pF[b][:, :]), inc=s_dve)
                return ("s_dve", v)

            if tile == 0:
                R.wait("sync", s_wst, transform_stores_all)
                R.wait("sync", s_cw["q"], cast_done["q"])
            q_ready = proj(sq_d, evac_q)
            if tile == 0:
                R.wait("sync", s_cw["k"], cast_done["k"])
            k_ready = proj(sk_d, evac_k)
            if tile == 0:
                R.wait("sync", s_cw["v"], cast_done["v"])
            v_ready = proj(sv_d, evac_v)
            if n_phases <= 2:
                dbg_stores()
                continue

            vt_done = {}
            vt_evac = {}
            ti = 0
            for oc in range(DC):
                R.wait("tensor", s_dve, v_ready[oc][1])
                for tb in range(NB):
                    b = ti % 2
                    if ti >= 2:
                        R.wait("tensor", s_dve, vt_evac[ti - 2])
                    vt_done[ti] = R.op("tensor", lambda e, b=b, oc=oc, tb=tb: e.transpose(
                        pT[b][:, 0:128], af_sb[:, oc * T + tb * 128: oc * T + (tb + 1) * 128],
                        ident[:, :]), inc=s_pe)
                    R.wait("vector", s_pe, vt_done[ti])
                    vt_evac[ti] = R.op("vector", lambda e, b=b, oc=oc, tb=tb: e.tensor_copy(
                        v_cur[:, tb * D + oc * 128: tb * D + (oc + 1) * 128], pT[b][:, 0:128]),
                        inc=s_dve)
                    ti += 1
            vt_all = vt_evac[ti - 1]
            if n_phases <= 3:
                dbg_stores()
                continue

            SCB = (4, 5, 3)
            AVB = (2, 0)
            peS, peT_, peP = {}, {}, {}
            dveM, dveE = {}, {}
            actE, actA, actTS = {}, {}, {}
            q_all = q_ready[DC - 1][1]
            k_all = k_ready[DC - 1][1]

            def hb_params(j):
                h, i = j // NB, j % NB
                oc, po = h // HPC, (h % HPC) * DK
                return i, h, oc, po

            def emit_SC(j):
                i, h, oc, po = hb_params(j)
                b = SCB[j % 3]
                if j >= 3:
                    R.wait("tensor", s_act, actE[j - 3])
                else:
                    bank_wait("tensor", b)
                if j == 0:
                    R.wait("tensor", s_act, q_all)
                    R.wait("tensor", s_dve, k_all)
                    R.wait("tensor", s_pool, carry_pool[0])
                msel = m0 if (seq_start and i == 0) else mG
                qs = q_sb[po:po + DK, oc * T + i * 128: oc * T + (i + 1) * 128]
                if i == 0:
                    kp = k_prev[po:po + DK, oc * 128:(oc + 1) * 128]
                    kc = k_cur[po:po + DK, oc * T: oc * T + 128]
                    R.op("tensor", lambda e, b=b, qs=qs, kp=kp: e.matmul(
                        pF[b][:, 0:128], qs, kp, start=True, stop=False))
                    R.op("tensor", lambda e, b=b, qs=qs, kc=kc: e.matmul(
                        pF[b][:, 128:256], qs, kc, start=False, stop=False))
                else:
                    kw = k_cur[po:po + DK, oc * T + (i - 1) * 128: oc * T + (i + 1) * 128]
                    R.op("tensor", lambda e, b=b, qs=qs, kw=kw: e.matmul(
                        pF[b][:, 0:256], qs, kw, start=True, stop=False))
                peS[j] = R.op("tensor", lambda e, b=b, msel=msel: e.matmul(
                    pF[b][:, 0:256], ident[:, :], msel[:, :], start=False, stop=True),
                    inc=s_pe)

            def emit_TR(j):
                b = j % 2
                R.wait("tensor", s_act, actTS[j])
                if j >= 2:
                    R.wait("tensor", s_dve, dveE[j - 2])
                R.op("tensor", lambda e, b=b: e.transpose(
                    pT[b][:, 0:128], pb_sb[b][:, 0:128], ident[:, :]))
                peT_[j] = R.op("tensor", lambda e, b=b: e.transpose(
                    pT[b][:, 128:256], pb_sb[b][:, 128:256], ident[:, :]), inc=s_pe)

            def emit_PV(j):
                i, h, oc, po = hb_params(j)
                ab = AVB[j % 2]
                R.wait("tensor", s_dve, dveE[j])
                if j == 0:
                    R.wait("tensor", s_dve, vt_all)
                    R.wait("tensor", s_pool, carry_pool[1])
                if j >= 2:
                    R.wait("tensor", s_act, actA[j - 2])
                else:
                    bank_wait("tensor", ab)
                v0 = (v_prev[:, h * DK: h * DK + DK] if i == 0
                      else v_cur[:, (i - 1) * D + h * DK: (i - 1) * D + h * DK + DK])
                v1 = v_cur[:, i * D + h * DK: i * D + h * DK + DK]
                R.op("tensor", lambda e, ab=ab, v0=v0, b=j % 2: e.matmul(
                    pF[ab][0:DK, 0:128], v0, ptT[b][:, 0:128], start=True, stop=False))
                peP[j] = R.op("tensor", lambda e, ab=ab, v1=v1, b=j % 2: e.matmul(
                    pF[ab][0:DK, 0:128], v1, ptT[b][:, 128:256], start=False, stop=True),
                    inc=s_pe)

            def emit_MAX(j):
                b = j % 3
                R.wait("vector", s_pe, peS[j])
                if j >= 3:
                    R.wait("vector", s_act, actE[j - 3])
                dveM[j] = R.op("vector", lambda e, b=b: e.reduce_max(
                    nmx[b][:, :], pF[SCB[b]][:, 0:256], axis=AX.X, negate=True), inc=s_dve)

            dveR = {}

            def emit_RECIP(j):
                b = j % 3
                R.wait("vector", s_act, actE[j])
                if j >= 3:
                    R.wait("vector", s_act, actTS[j - 3])
                dveR[j] = R.op("vector", lambda e, b=b: e.reciprocal(
                    rsr[b][:, :], smr[b][:, :]), inc=s_dve)

            def emit_TSc(j):
                b = j % 3
                pb = j % 2
                R.wait("scalar", s_dve, dveR[j])
                if j >= 2:
                    R.wait("scalar", s_pe, peT_[j - 2])
                actTS[j] = R.op("scalar", lambda e, b=b, pb=pb: e.activation(
                    pb_sb[pb][:, :], p_sb[b][:, :], AF.Copy, scale=rsr[b][:, 0:1]),
                    inc=s_act)

            def emit_PTE(j):
                b = j % 2
                R.wait("vector", s_pe, peT_[j])
                if j >= 2:
                    R.wait("vector", s_pe, peP[j - 2])
                dveE[j] = R.op("vector", lambda e, b=b: e.tensor_copy(
                    ptT[b][:, :], pT[b][:, :]), inc=s_dve)

            def emit_EXP(j):
                b = j % 3
                R.wait("scalar", s_dve, dveM[j])
                if j >= 3:
                    R.wait("scalar", s_act, actTS[j - 3], force=True)
                actE[j] = R.op("scalar", lambda e, b=b: e.activation(
                    p_sb[b][:, :], pF[SCB[b]][:, 0:256], AF.Exp, bias=nmx[b][:, 0:1],
                    accum_out=smr[b][:, 0:1]), inc=s_act)

            def emit_AVE(j):
                i, h, oc, po = hb_params(j)
                ab = AVB[j % 2]
                R.wait("scalar", s_pe, peP[j])
                actA[j] = R.op("scalar", lambda e, ab=ab, oc=oc, po=po, i=i: e.activation(
                    af_sb[po:po + DK, oc * T + i * 128: oc * T + (i + 1) * 128],
                    pF[ab][0:DK, 0:128], AF.Copy), inc=s_act)

            for j in range(NHB + 6):
                if j < NHB:
                    emit_SC(j)
                if 0 <= j - 3 < NHB:
                    emit_PTE(j - 3)
                if 0 <= j - 5 < NHB:
                    emit_AVE(j - 5)
                if 0 <= j - 1 < NHB:
                    emit_RECIP(j - 1)
                if j < NHB:
                    emit_MAX(j)
                if 0 <= j - 1 < NHB:
                    emit_TSc(j - 1)
                if j < NHB:
                    emit_EXP(j)
                if 0 <= j - 2 < NHB:
                    emit_TR(j - 2)
                if 0 <= j - 4 < NHB:
                    emit_PV(j - 4)

            for k in range(3):
                mm_bank_free[SCB[(NHB - 1 - k) % 3]] = ("s_act", actE[NHB - 1 - k])
            for k in range(2):
                mm_bank_free[AVB[(NHB - 1 - k) % 2]] = ("s_act", actA[NHB - 1 - k])
            attn_all = actA[NHB - 1]

            if (tile + 1) % TPS != 0:
                R.wait("gpsimd", s_pe, peS[NHB - 1])
                kc3 = k_cur[:, :].rearrange("p (oc t) -> p oc t", t=T)
                kp3 = k_prev[:, :].rearrange("p (oc t) -> p oc t", t=128)
                carry_pool[0] = R.op("gpsimd", lambda e, kc3=kc3, kp3=kp3: e.tensor_copy(
                    kp3[:, :, :], kc3[:, :, T - 128:T]), inc=s_pool)
                R.wait("gpsimd", s_pe, peP[NHB - 1])
                carry_pool[1] = R.op("gpsimd", lambda e: e.tensor_copy(
                    v_prev[:, :], v_cur[:, (NB - 1) * D: NB * D]), inc=s_pool)
            if n_phases <= 4:
                dbg_stores()
                continue

            wo_evac = {}
            if tile == 0:
                R.wait("sync", s_cw["o"], cast_done["o"])
            for oc in range(DC):
                seq, slot, wv_ = stream_stripe(so_d[oc])
                b = oc % 2
                R.wait("tensor", wv_[0], wv_[1])
                bank_wait("tensor", b)
                if oc == 0:
                    R.wait("tensor", s_act, attn_all)
                for dc2 in range(DC):
                    pev = R.op("tensor", lambda e, b=b, slot=slot, dc2=dc2: e.matmul(
                        pF[b][:, :], slot[:, dc2 * 128:(dc2 + 1) * 128],
                        af_sb[:, dc2 * T:(dc2 + 1) * T],
                        start=(dc2 == 0), stop=(dc2 == DC - 1)), inc=s_pe)
                mark_consumed(seq, pev)
                R.wait("vector", s_pe, pev)
                wo_evac[oc] = R.op("vector", lambda e, oc=oc, b=b: e.tensor_tensor(
                    x_sb[:, oc * T:(oc + 1) * T], x_sb[:, oc * T:(oc + 1) * T],
                    pF[b][:, :], ALU.add), inc=s_dve)
                mm_bank_free[b] = ("s_dve", wo_evac[oc])
            if n_phases <= 5:
                dbg_stores()
                continue

            hid_done = {}
            silu_done = {}
            if tile == 0:
                R.wait("sync", s_cw["1"], cast_done["1"])
                R.wait("sync", s_cw["3"], cast_done["3"])
            for hc in range(HC):
                gb, lb = ((4, 5), (2, 3))[hc % 2]
                seq1, slot1, wv1 = stream_stripe(s1_d[hc])
                seq3, slot3, wv3 = stream_stripe(s3_d[hc])
                if hc == DC:
                    R.wait("vector", s_pool, carry_pool[0])
                if hc == 3 * DC:
                    R.wait("vector", s_pool, carry_pool[1])
                R.wait("tensor", wv1[0], wv1[1])
                bank_wait("tensor", gb)
                for dc2 in range(DC):
                    peg = R.op("tensor", lambda e, gb=gb, slot1=slot1, dc2=dc2: e.matmul(
                        pF[gb][:, :], slot1[:, dc2 * 128:(dc2 + 1) * 128],
                        xn_sb[:, dc2 * T:(dc2 + 1) * T],
                        start=(dc2 == 0), stop=(dc2 == DC - 1)), inc=s_pe)
                mark_consumed(seq1, peg)
                R.wait("tensor", wv3[0], wv3[1])
                bank_wait("tensor", lb)
                for dc2 in range(DC):
                    pel = R.op("tensor", lambda e, lb=lb, slot3=slot3, dc2=dc2: e.matmul(
                        pF[lb][:, :], slot3[:, dc2 * 128:(dc2 + 1) * 128],
                        xn_sb[:, dc2 * T:(dc2 + 1) * T],
                        start=(dc2 == 0), stop=(dc2 == DC - 1)), inc=s_pe)
                mark_consumed(seq3, pel)
                R.wait("scalar", s_pe, peg)
                if hc >= 2:
                    R.wait("scalar", s_dve, hid_done[hc - 2])
                silu_done[hc] = R.op("scalar", lambda e, hc=hc, gb=gb: e.activation(
                    sg[hc % 2][:, :], pF[gb][:, :], AF.Silu), inc=s_act)
                mm_bank_free[gb] = ("s_act", silu_done[hc])
                R.wait("vector", s_pe, pel)
                R.wait("vector", s_act, silu_done[hc])
                hid_done[hc] = R.op("vector", lambda e, hc=hc, lb=lb: e.tensor_tensor(
                    hidden[:, hc * T:(hc + 1) * T], sg[hc % 2][:, :], pF[lb][:, :],
                    ALU.mult), inc=s_dve)
                mm_bank_free[lb] = ("s_dve", hid_done[hc])
            if n_phases <= 6:
                dbg_stores()
                continue

            if tile == 0:
                R.wait("sync", s_wst, transform_stores_all)
            for dc2 in range(DC):
                if w2_pe_prev[dc2 % 2] is not None:
                    R.wait("sync", s_pe, w2_pe_prev[dc2 % 2])
                wv_ = R.op("sync", lambda e, dc2=dc2: e.dma_start(
                    w2buf[dc2 % 2][:, :].rearrange("p (a b) -> p a b", b=128), s2_d[dc2]),
                    inc=s_w2b[dc2 % 2], inc_by=16)
                b = 4 + (dc2 % 2)
                R.wait("tensor", s_w2b[dc2 % 2], wv_)
                bank_wait("tensor", b)
                R.wait("tensor", s_dve, hid_done[HC - 1])
                for hc in range(HC):
                    pev = R.op("tensor", lambda e, b=b, dc2=dc2, hc=hc: e.matmul(
                        pF[b][:, :], w2buf[dc2 % 2][:, hc * 128:(hc + 1) * 128],
                        hidden[:, hc * T:(hc + 1) * T],
                        start=(hc == 0), stop=(hc == HC - 1)), inc=s_pe)
                w2_pe_prev[dc2 % 2] = pev
                R.wait("vector", s_pe, pev)
                ev = R.op("vector", lambda e, dc2=dc2, b=b: e.tensor_tensor(
                    x_sb[:, dc2 * T:(dc2 + 1) * T], x_sb[:, dc2 * T:(dc2 + 1) * T],
                    pF[b][:, :], ALU.add), inc=s_dve)
                mm_bank_free[b] = ("s_dve", ev)
                R.wait("gpsimd", s_dve, ev)
                R.op("gpsimd", lambda e, dc2=dc2, t0=t0: e.dma_start(
                    out_d[dc2][:, t0:t0 + T], x_sb[:, dc2 * T:(dc2 + 1) * T]),
                    inc=s_sc[dc2], inc_by=16)

        for dc2 in range(DC):
            R.wait("gpsimd", s_sc[dc2], s_sc[dc2].n)

        with nc.Block() as block:
            @block.tensor
            def _(t):
                for f in R.prog["tensor"]:
                    f(t)

            @block.vector
            def _(v):
                for f in R.prog["vector"]:
                    f(v)

            @block.scalar
            def _(s):
                for f in R.prog["scalar"]:
                    f(s)

            @block.sync
            def _(sy):
                for f in R.prog["sync"]:
                    f(sy)

            @block.gpsimd
            def _(g):
                for f in R.prog["gpsimd"]:
                    f(g)

    return nc


_CACHED = {}


def _get_program():
    if "nc" not in _CACHED:
        _CACHED["nc"] = build_program()
    return _CACHED["nc"]


def kernel(L0_attn_x, L0_attn_Wq, L0_attn_Wk, L0_attn_Wv, L0_attn_Wo,
           L0_ffn_W1, L0_ffn_W2, L0_ffn_W3):
    B, S, D = L0_attn_x.shape
    HF = L0_ffn_W1.shape[0]
    DC, HC = D // 128, HF // 128
    n_cores = 8
    bpc = B // n_cores

    def lhsT_layout(W, n_out):
        return np.ascontiguousarray(
            np.asarray(W).reshape(n_out, 128, DC, 128).transpose(0, 3, 2, 1))

    wq = lhsT_layout(L0_attn_Wq, DC)
    wk = lhsT_layout(L0_attn_Wk, DC)
    wv = lhsT_layout(L0_attn_Wv, DC)
    wo = lhsT_layout(L0_attn_Wo, DC)
    w1 = lhsT_layout(L0_ffn_W1, HC)
    w3 = lhsT_layout(L0_ffn_W3, HC)
    w2 = np.ascontiguousarray(
        np.asarray(L0_ffn_W2).reshape(DC, 128, HC, 128).transpose(0, 3, 2, 1))

    x = np.asarray(L0_attn_x)
    in_maps = []
    for core in range(n_cores):
        xc = np.ascontiguousarray(
            x[core * bpc:(core + 1) * bpc].transpose(2, 0, 1).reshape(D, bpc * S)
            .reshape(DC, 128, bpc * S))
        in_maps.append({"x": xc, "wq": wq, "wk": wk, "wv": wv, "wo": wo,
                        "w1": w1, "w3": w3, "w2": w2})

    nc = _get_program()
    res = run_bass_kernel_spmd(nc, in_maps, core_ids=list(range(n_cores)))

    out = np.empty((B, S, D), dtype=np.float32)
    for core in range(n_cores):
        oc_ = np.asarray(res.results[core]["out"]).reshape(D, bpc, S)
        out[core * bpc:(core + 1) * bpc] = oc_.transpose(1, 2, 0)
    return out


if __name__ == "__main__":
    import json
    rng = np.random.default_rng(0)
    print("building program...")
    nc = _get_program()
    print("n_instructions:", nc.n_instructions())


# version 09
# speedup vs baseline: 1.0139x; 1.0139x over previous
import numpy as np
import ml_dtypes
from contextlib import ExitStack

import concourse.bass as bass
import concourse.mybir as mybir
from concourse.bass_utils import run_bass_kernel_spmd

F32 = mybir.dt.float32
BF16 = mybir.dt.bfloat16
AF = mybir.ActivationFunctionType
ALU = mybir.AluOpType
AX = mybir.AxisListType

ENGS = ("tensor", "vector", "scalar", "sync", "gpsimd")


def hadamard(n):
    H = np.array([[1.0]])
    while H.shape[0] < n:
        H = np.block([[H, H], [H, -H]])
    return H


class Sem:

    def __init__(self, handle, name):
        self.h = handle
        self.name = name
        self.n = 0


class Rec:

    def __init__(self):
        self.prog = {e: [] for e in ENGS}
        self.mark = {e: {} for e in ENGS}

    def wait(self, eng, sem, val, force=False):
        if val <= 0:
            return
        if not force and self.mark[eng].get(sem.name, -1) >= val:
            return
        if self.mark[eng].get(sem.name, -1) < val:
            self.mark[eng][sem.name] = val
        self.prog[eng].append(lambda e, s=sem.h, v=val: e.wait_ge(s, v))

    def op(self, eng, fn, inc=None, inc_by=1):
        if inc is None:
            self.prog[eng].append(fn)
            return None
        inc.n += inc_by
        v = inc.n
        self.prog[eng].append(lambda e, f=fn, s=inc.h, k=inc_by: f(e).then_inc(s, k))
        if inc_by == 1 and self.mark[eng].get(inc.name, -1) < v:
            self.mark[eng][inc.name] = v
        return v


def build_program(D=2048, HF=8192, T=512, NTOK=4096, SEQ=2048, HEADS=32, DK=64,
                  n_phases=7):
    DC = D // 128
    HC = HF // 128
    NB = T // 128
    NT = NTOK // T
    TPS = SEQ // T
    HPC = 128 // DK
    NHB = NB * HEADS

    nc = bass.Bass("TRN2", target_bir_lowering=False)

    x_d = nc.dram_tensor("x", [DC, 128, NTOK], F32, kind="ExternalInput").ap()
    wq_d = nc.dram_tensor("wq", [DC, 128, DC, 128], F32, kind="ExternalInput").ap()
    wk_d = nc.dram_tensor("wk", [DC, 128, DC, 128], F32, kind="ExternalInput").ap()
    wv_d = nc.dram_tensor("wv", [DC, 128, DC, 128], F32, kind="ExternalInput").ap()
    wo_d = nc.dram_tensor("wo", [DC, 128, DC, 128], F32, kind="ExternalInput").ap()
    w1_d = nc.dram_tensor("w1", [HC, 128, DC, 128], F32, kind="ExternalInput").ap()
    w3_d = nc.dram_tensor("w3", [HC, 128, DC, 128], F32, kind="ExternalInput").ap()
    w2_d = nc.dram_tensor("w2", [DC, 128, HC, 128], F32, kind="ExternalInput").ap()
    out_d = nc.dram_tensor("out", [DC, 128, NTOK], F32, kind="ExternalOutput").ap()

    sq_d = nc.dram_tensor("sq", [DC, 128, DC, 128], BF16, kind="Internal").ap()
    sk_d = nc.dram_tensor("sk", [DC, 128, DC, 128], BF16, kind="Internal").ap()
    sv_d = nc.dram_tensor("sv", [DC, 128, DC, 128], BF16, kind="Internal").ap()
    so_d = nc.dram_tensor("so", [DC, 128, DC, 128], BF16, kind="Internal").ap()
    s1_d = nc.dram_tensor("s1", [HC, 128, DC, 128], BF16, kind="Internal").ap()
    s3_d = nc.dram_tensor("s3", [HC, 128, DC, 128], BF16, kind="Internal").ap()
    s2_d = nc.dram_tensor("s2", [DC, 128, HC, 128], BF16, kind="Internal").ap()

    ident_np = np.eye(128, dtype=ml_dtypes.bfloat16)
    Hb_np = hadamard(128).astype(ml_dtypes.bfloat16)
    r = np.arange(128)[:, None]
    c = np.arange(256)[None, :]
    validG = (c >= r) & (c <= r + 128)
    mG_np = np.where(validG, 0.0, -1e9).astype(ml_dtypes.bfloat16)
    m0_np = np.where(validG & (c >= 128), 0.0, -1e9).astype(ml_dtypes.bfloat16)
    onesn_np = np.full((128, 1), -1.0 / D, dtype=np.float32)
    onesp_np = np.full((128, 1), 1.0 / D, dtype=ml_dtypes.bfloat16)
    ones1_np = np.ones((1, 128), dtype=np.float32)
    eps_np = np.full((1, 1), 1e-5, dtype=np.float32)

    ident_d = nc.inline_tensor(ident_np, name="identC").ap()
    Hb_d = nc.inline_tensor(Hb_np, name="HbC").ap()
    mG_d = nc.inline_tensor(mG_np, name="mGC").ap()
    m0_d = nc.inline_tensor(m0_np, name="m0C").ap()
    onesn_d = nc.inline_tensor(onesn_np, name="onesnC").ap()
    onesp_d = nc.inline_tensor(onesp_np, name="onespC").ap()
    ones1_d = nc.inline_tensor(ones1_np, name="ones1C").ap()
    eps_d = nc.inline_tensor(eps_np, name="epsC").ap()

    R = Rec()

    with ExitStack() as ctx:
        E = ctx.enter_context

        x_sb = E(nc.sbuf_tensor("x_sb", [128, DC * T], F32))
        xn_sb = E(nc.sbuf_tensor("xn_sb", [128, DC * T], BF16))
        k_prev = E(nc.sbuf_tensor("k_prev", [128, DC * 128], BF16))
        v_prev = E(nc.sbuf_tensor("v_prev", [128, D], BF16))
        ident = E(nc.sbuf_tensor("ident", [128, 128], BF16))
        Hb = E(nc.sbuf_tensor("Hb", [128, 128], BF16))
        mm_bank_free = {}
        mG = E(nc.sbuf_tensor("mG", [128, 256], BF16))
        m0 = E(nc.sbuf_tensor("m0", [128, 256], BF16))
        onesn = E(nc.sbuf_tensor("onesn", [128, 1], F32))
        onesp = E(nc.sbuf_tensor("onesp", [128, 1], BF16))
        ones1 = E(nc.sbuf_tensor("ones1", [1, 128], F32))
        eps_sb = E(nc.sbuf_tensor("eps_sb", [1, 1], F32))
        meanrow = E(nc.sbuf_tensor("meanrow", [1, T], F32))
        m2row = E(nc.sbuf_tensor("m2row", [1, T], F32))
        varrow = E(nc.sbuf_tensor("varrow", [1, T], F32))
        lnrow = E(nc.sbuf_tensor("lnrow", [1, T], F32))
        rstdrow = E(nc.sbuf_tensor("rstdrow", [1, T], F32))
        sqb = [E(nc.sbuf_tensor(f"sqb{i}", [128, T], BF16)) for i in range(2)]
        xctmp = E(nc.sbuf_tensor("xctmp", [128, T], F32))
        nmx = [E(nc.sbuf_tensor(f"nmx{i}", [128, 1], F32)) for i in range(3)]
        smr = [E(nc.sbuf_tensor(f"smr{i}", [128, 1], F32)) for i in range(3)]
        rsr = [E(nc.sbuf_tensor(f"rsr{i}", [128, 1], F32)) for i in range(3)]
        p_sb = [E(nc.sbuf_tensor(f"p_sb{i}", [128, 256], F32)) for i in range(3)]
        pb_sb = [E(nc.sbuf_tensor(f"pb_sb{i}", [128, 256], BF16)) for i in range(2)]
        ptT = [E(nc.sbuf_tensor(f"ptT{i}", [128, 256], BF16)) for i in range(2)]
        sg = [E(nc.sbuf_tensor(f"sg{i}", [128, T], F32)) for i in range(2)]

        sems = {}
        for name in ("s_pe", "s_dve", "s_act", "s_pool", "s_wst", "s_wld",
                     "s_x", "s_st", "s_cvt"):
            sems[name] = Sem(E(nc.semaphore(name)), name)
        s_pe, s_dve, s_act, s_pool = (sems[k] for k in ("s_pe", "s_dve", "s_act", "s_pool"))
        s_wst, s_wld, s_x, s_st, s_cvt = (
            sems[k] for k in ("s_wst", "s_wld", "s_x", "s_st", "s_cvt"))
        NRING = 6
        s_rg = []
        for i in range(NRING):
            s_rg.append(Sem(E(nc.semaphore(f"s_rg{i}")), f"s_rg{i}"))
            sems[f"s_rg{i}"] = s_rg[i]
        s_w2b = []
        for i in range(2):
            s_w2b.append(Sem(E(nc.semaphore(f"s_w2b{i}")), f"s_w2b{i}"))
            sems[f"s_w2b{i}"] = s_w2b[i]
        s_xc = []
        s_sc = []
        for i in range(DC):
            s_xc.append(Sem(E(nc.semaphore(f"s_xc{i}")), f"s_xc{i}"))
            sems[f"s_xc{i}"] = s_xc[i]
            s_sc.append(Sem(E(nc.semaphore(f"s_sc{i}")), f"s_sc{i}"))
            sems[f"s_sc{i}"] = s_sc[i]

        s_cst = Sem(E(nc.semaphore("s_cst")), "s_cst")
        sems["s_cst"] = s_cst
        s_cw = {}
        for w in ("q", "k", "v", "o", "1", "3"):
            s_cw[w] = Sem(E(nc.semaphore(f"s_cw{w}")), f"s_cw{w}")
            sems[f"s_cw{w}"] = s_cw[w]
        s_wl3 = []
        for i in range(4):
            s_wl3.append(Sem(E(nc.semaphore(f"s_wl3{i}")), f"s_wl3{i}"))
            sems[f"s_wl3{i}"] = s_wl3[i]

        for dst, src in ((ident, ident_d), (Hb, Hb_d), (mG, mG_d), (m0, m0_d),
                         (onesn, onesn_d), (onesp, onesp_d), (ones1, ones1_d),
                         (eps_sb, eps_d)):
            R.op("sync", lambda e, d=dst, s=src: e.dma_start(d[:, :], s), inc=s_cst, inc_by=16)
        cst_all = s_cst.n

        cast_specs = {"q": (wq_d, sq_d, DC), "k": (wk_d, sk_d, DC),
                      "v": (wv_d, sv_d, DC), "o": (wo_d, so_d, DC),
                      "1": (w1_d, s1_d, HC), "3": (w3_d, s3_d, HC)}
        cast_done = {}

        def emit_casts(name):
            src, dst, n = cast_specs[name]
            for i in range(n):
                R.op("gpsimd", lambda e, s=src, d=dst, i=i: e.dma_start(
                    d[i].rearrange("p a b -> p (a b)"), s[i].rearrange("p a b -> p (a b)")),
                    inc=s_cw[name], inc_by=16)
            cast_done[name] = s_cw[name].n

        with nc.sbuf_tensor("slabX0", [128, HC * 128], BF16) as X0, \
             nc.sbuf_tensor("slabX1", [128, HC * 128], BF16) as X1, \
             nc.sbuf_tensor("slabX2", [128, HC * 128], BF16) as X2, \
             nc.sbuf_tensor("slabX3", [128, HC * 128], BF16) as X3, \
             nc.sbuf_tensor("slabS", [128, HC * 128], BF16) as Ssl, \
             nc.sbuf_tensor("outsl0", [128, HC * 128], BF16) as O0, \
             nc.sbuf_tensor("outsl1", [128, HC * 128], BF16) as O1, \
             nc.psum_tensor("pW0", [128, 512], F32) as pW0, \
             nc.psum_tensor("pW1", [128, 512], F32) as pW1:
            X = [X0, X1, X2, X3]
            OSL = [O0, O1]
            pW = [pW0, pW1]
            n_stage = int(np.log2(HC))
            mm_last = {}
            evac_last = {}
            store_done = {}

            def emit_slab_load(sl):
                if sl >= DC:
                    return
                if sl >= 4:
                    R.wait("gpsimd", s_pe, mm_last[sl - 4])
                R.op("gpsimd", lambda e, sl=sl: e.dma_start(
                    X[sl % 4][:, :], w2_d[sl].rearrange("p a b -> p (a b)")),
                    inc=s_wl3[sl % 4], inc_by=16)

            evac_g = {}
            for sl in range(min(4, DC)):
                emit_slab_load(sl)
            emit_casts("q")
            emit_casts("k")
            cast_after = {0: "v", 2: "o"}

            fin_was_S = False
            for sl in range(DC):
                R.wait("vector", s_wl3[sl % 4], 16 * (sl // 4 + 1))
                if fin_was_S and sl > 0:
                    R.wait("vector", s_pe, mm_last[sl - 1])
                src, dst = X[sl % 4], Ssl
                vlast = None
                for s in range(n_stage):
                    span = (2 ** s) * 128
                    a_s = src[:, :].rearrange("p (g two x) -> p g two x", two=2, x=span)
                    a_d = dst[:, :].rearrange("p (g two x) -> p g two x", two=2, x=span)
                    if vlast is not None:
                        R.wait("vector", s_dve, vlast, force=True)
                    R.op("vector", lambda e, a_d=a_d, a_s=a_s: e.tensor_tensor(
                        a_d[:, :, 0, :], a_s[:, :, 0, :], a_s[:, :, 1, :], ALU.add),
                        inc=s_dve)
                    vlast = R.op("vector", lambda e, a_d=a_d, a_s=a_s: e.tensor_tensor(
                        a_d[:, :, 1, :], a_s[:, :, 0, :], a_s[:, :, 1, :], ALU.subtract),
                        inc=s_dve)
                    src, dst = dst, src
                bfly_done = vlast
                fin = src
                fin_was_S = fin is Ssl
                R.wait("tensor", s_dve, bfly_done)
                if sl == 0:
                    R.wait("tensor", s_cst, cst_all)
                for g in range(HC // 4):
                    b = g % 2
                    if sl * (HC // 4) + g >= 2:
                        R.wait("tensor", s_act, evac_g[(sl * (HC // 4) + g) - 2])
                    for cc in range(4):
                        cch = g * 4 + cc
                        mm_last[sl] = R.op("tensor", lambda e, b=b, fin=fin, cch=cch, cc=cc: e.matmul(
                            pW[b][:, cc * 128:(cc + 1) * 128], Hb[:, :],
                            fin[:, cch * 128:(cch + 1) * 128],
                            start=(cc == 0), stop=(cc == 3)), inc=s_pe)
                    if g == 0 and sl >= 2:
                        R.wait("scalar", s_wst, store_done[sl - 2])
                    R.wait("scalar", s_pe, mm_last[sl])
                    evac_g[sl * (HC // 4) + g] = R.op(
                        "scalar", lambda e, b=b, g=g, sl=sl: e.activation(
                            OSL[sl % 2][:, g * 512:(g + 1) * 512], pW[b][:, :],
                            AF.Copy, scale=float(1.0 / np.sqrt(HF))), inc=s_act)
                evac_last[sl] = evac_g[sl * (HC // 4) + (HC // 4) - 1]
                R.wait("scalar", s_act, evac_last[sl], force=True)
                store_done[sl] = R.op("scalar", lambda e, sl=sl: e.dma_start(
                    s2_d[sl].rearrange("p a b -> p (a b)"), OSL[sl % 2][:, :]),
                    inc=s_wst, inc_by=16)
                emit_slab_load(sl + 4)
                if sl in cast_after:
                    emit_casts(cast_after[sl])

            for w in ("q", "k", "v", "o"):
                if w not in cast_done:
                    emit_casts(w)
            defer_13 = [w for w in ("1", "3") if w not in cast_done]

            transform_stores_all = s_wst.n
            mm_bank_free[0] = ("s_act", evac_last[DC - 1])
            if DC >= 2:
                mm_bank_free[1] = ("s_act", evac_last[DC - 2])

        alias = E(nc.sbuf_tensor("alias", [128, 4 * DC * T], BF16))
        q_sb = alias[:, 0 * DC * T:1 * DC * T]
        k_cur = alias[:, 1 * DC * T:2 * DC * T]
        af_sb = alias[:, 2 * DC * T:3 * DC * T]
        v_cur = alias[:, 3 * DC * T:4 * DC * T]
        hidden = alias[:, :]
        assert HC * T == 4 * DC * T
        ring = [E(nc.sbuf_tensor(f"ring{i}", [128, DC * 128], BF16)) for i in range(NRING)]
        w2buf = [E(nc.sbuf_tensor(f"w2buf{i}", [128, HC * 128], BF16)) for i in range(2)]

        pF = [E(nc.psum_tensor(f"pF{i}", [128, T], F32)) for i in range(6)]
        pT = [E(nc.psum_tensor(f"pT{i}", [128, 256], BF16)) for i in range(2)]


        ring_i = [0]
        ring_consumed = {}

        def stream_stripe(src_ap):
            seq = ring_i[0]
            ring_i[0] += 1
            slot = ring[seq % NRING]
            sem = s_rg[seq % NRING]
            if seq - NRING in ring_consumed:
                R.wait("sync", s_pe, ring_consumed[seq - NRING])
            v = R.op("sync", lambda e, slot=slot, s=src_ap: e.dma_start(
                slot[:, :].rearrange("p (a b) -> p a b", b=128), s), inc=sem, inc_by=16)
            return seq, slot, (sem, v)

        def mark_consumed(seq, pe_val):
            ring_consumed[seq] = pe_val

        store_v = {}

        def bank_wait(eng, bank):
            if bank in mm_bank_free:
                sname, v = mm_bank_free[bank]
                R.wait(eng, sems[sname], v)

        last_xn = [0]
        carry_pool = [0, 0]
        w2_pe_prev = {0: None, 1: None}

        for tile in range(NT):
            t0 = tile * T
            seq_start = (tile % TPS) == 0

            if tile == 0:
                for e in ("tensor", "scalar", "vector"):
                    R.wait(e, s_cst, cst_all)
                for dc2 in range(DC):
                    R.op("sync", lambda e, dc2=dc2, t0=t0: e.dma_start(
                        x_sb[:, dc2 * T:(dc2 + 1) * T], x_d[dc2][:, t0:t0 + T]),
                        inc=s_x, inc_by=16)
            else:
                for dc2 in range(DC):
                    R.wait("gpsimd", s_sc[dc2], 16 * tile)
                    R.op("gpsimd", lambda e, dc2=dc2, t0=t0: e.dma_start(
                        x_sb[:, dc2 * T:(dc2 + 1) * T], x_d[dc2][:, t0:t0 + T]),
                        inc=s_xc[dc2], inc_by=16)
            x_chunk = 16 * tile

            if n_phases <= 0:
                for dc2 in range(DC):
                    if tile == 0:
                        R.wait("gpsimd", s_x, 16 * DC)
                    else:
                        R.wait("gpsimd", s_xc[dc2], x_chunk)
                    R.op("gpsimd", lambda e, dc2=dc2, t0=t0: e.dma_start(
                        out_d[dc2][:, t0:t0 + T], x_sb[:, dc2 * T:(dc2 + 1) * T]),
                        inc=s_sc[dc2], inc_by=16)
                continue

            if seq_start:
                R.wait("gpsimd", s_pe, s_pe.n)
                carry_pool[0] = R.op("gpsimd", lambda e: e.memset(k_prev[:, :], 0.0), inc=s_pool)
                carry_pool[1] = R.op("gpsimd", lambda e: e.memset(v_prev[:, :], 0.0), inc=s_pool)

            def dbg_stores0(t0=t0, tile=tile, x_chunk=x_chunk):
                for sname in ("s_pe", "s_dve", "s_act", "s_pool"):
                    R.wait("gpsimd", sems[sname], sems[sname].n)
                for dc2 in range(DC):
                    if tile == 0:
                        R.wait("gpsimd", s_x, 16 * DC)
                    else:
                        R.wait("gpsimd", s_xc[dc2], x_chunk)
                    R.op("gpsimd", lambda e, dc2=dc2, t0=t0: e.dma_start(
                        out_d[dc2][:, t0:t0 + T], x_sb[:, dc2 * T:(dc2 + 1) * T]),
                        inc=s_sc[dc2], inc_by=16)

            if n_phases == 11:
                dbg_stores0()
                continue
            bank_wait("tensor", 0)
            bank_wait("tensor", 1)
            bank_wait("tensor", 2)
            bank_wait("tensor", 3)
            for dc2 in range(DC):
                if tile == 0:
                    R.wait("tensor", s_x, 16 * DC)
                else:
                    R.wait("tensor", s_xc[dc2], x_chunk)
                pe_mean = R.op("tensor", lambda e, dc2=dc2: e.matmul(
                    pF[0][0:1, :], onesn[:, :], x_sb[:, dc2 * T:(dc2 + 1) * T],
                    start=(dc2 == 0), stop=(dc2 == DC - 1)), inc=s_pe)
            sq_pe = {}
            for dc2 in range(DC):
                if tile == 0:
                    R.wait("scalar", s_x, 16 * DC)
                else:
                    R.wait("scalar", s_xc[dc2], x_chunk)
                if dc2 >= 2:
                    R.wait("scalar", s_pe, sq_pe[dc2 - 2])
                aqv = R.op("scalar", lambda e, dc2=dc2: e.activation(
                    sqb[dc2 % 2][:, :], x_sb[:, dc2 * T:(dc2 + 1) * T], AF.Square), inc=s_act)
                R.wait("tensor", s_act, aqv)
                sq_pe[dc2] = R.op("tensor", lambda e, dc2=dc2: e.matmul(
                    pF[1][0:1, :], onesp[:, :], sqb[dc2 % 2][:, :],
                    start=(dc2 == 0), stop=(dc2 == DC - 1)), inc=s_pe)
            if n_phases == 12:
                dbg_stores0()
                continue
            R.wait("vector", s_pe, pe_mean)
            v_mr = R.op("vector", lambda e: e.tensor_copy(meanrow[:, :], pF[0][0:1, :]), inc=s_dve)
            R.wait("scalar", s_dve, v_mr)
            a_m2 = R.op("scalar", lambda e: e.activation(m2row[:, :], meanrow[:, :], AF.Square),
                        inc=s_act)
            R.wait("vector", s_pe, sq_pe[DC - 1])
            R.wait("vector", s_act, a_m2)
            v_var = R.op("vector", lambda e: e.tensor_tensor(
                varrow[:, :], pF[1][0:1, :], m2row[:, :], ALU.subtract), inc=s_dve)
            R.wait("scalar", s_dve, v_var)
            a_ln = R.op("scalar", lambda e: e.activation(
                lnrow[:, :], varrow[:, :], AF.Ln, bias=eps_sb[0:1, 0:1]), inc=s_act)
            R.wait("scalar", s_act, a_ln, force=True)
            a_rstd = R.op("scalar", lambda e: e.activation(
                rstdrow[:, :], lnrow[:, :], AF.Exp, scale=-0.5), inc=s_act)
            if n_phases == 13:
                dbg_stores0()
                continue
            R.wait("tensor", s_dve, v_mr)
            pe_bc1 = R.op("tensor", lambda e: e.matmul(
                pF[2][:, :], ones1[:, :], meanrow[:, :], start=True, stop=True), inc=s_pe)
            R.wait("tensor", s_act, a_rstd)
            pe_bc2 = R.op("tensor", lambda e: e.matmul(
                pF[3][:, :], ones1[:, :], rstdrow[:, :], start=True, stop=True), inc=s_pe)
            if n_phases == 14:
                dbg_stores0()
                continue
            R.wait("vector", s_pe, pe_bc2)
            xn_done = {}
            for dc2 in range(DC):
                if dc2 > 0:
                    R.wait("vector", s_dve, last_xn[0], force=True)
                vadd = R.op("vector", lambda e, dc2=dc2: e.tensor_tensor(
                    xctmp[:, :], x_sb[:, dc2 * T:(dc2 + 1) * T], pF[2][:, :], ALU.add),
                    inc=s_dve)
                R.wait("vector", s_dve, vadd, force=True)
                last_xn[0] = R.op("vector", lambda e, dc2=dc2: e.tensor_tensor(
                    xn_sb[:, dc2 * T:(dc2 + 1) * T], xctmp[:, :], pF[3][:, :], ALU.mult),
                    inc=s_dve)
                xn_done[dc2] = last_xn[0]
            mm_bank_free[0] = ("s_dve", v_mr)
            mm_bank_free[1] = ("s_dve", v_var)
            mm_bank_free[2] = ("s_dve", last_xn[0])
            mm_bank_free[3] = ("s_dve", last_xn[0])

            def dbg_stores(t0=t0, tile=tile):
                for sname in ("s_pe", "s_dve", "s_act", "s_pool"):
                    R.wait("gpsimd", sems[sname], sems[sname].n)
                for dc2 in range(DC):
                    R.op("gpsimd", lambda e, dc2=dc2, t0=t0: e.dma_start(
                        out_d[dc2][:, t0:t0 + T], x_sb[:, dc2 * T:(dc2 + 1) * T]),
                        inc=s_sc[dc2], inc_by=16)

            if n_phases <= 1:
                dbg_stores()
                continue

            def proj(scr, evac):
                ready = []
                for oc in range(DC):
                    seq, slot, wv_ = stream_stripe(scr[oc])
                    b = 4 + (oc % 2)
                    R.wait("tensor", wv_[0], wv_[1])
                    bank_wait("tensor", b)
                    for dc2 in range(DC):
                        R.wait("tensor", s_dve, xn_done[dc2])
                        pev = R.op("tensor", lambda e, b=b, slot=slot, dc2=dc2: e.matmul(
                            pF[b][:, :], slot[:, dc2 * 128:(dc2 + 1) * 128],
                            xn_sb[:, dc2 * T:(dc2 + 1) * T],
                            start=(dc2 == 0), stop=(dc2 == DC - 1)), inc=s_pe)
                    mark_consumed(seq, pev)
                    sname, val = evac(oc, b, pev)
                    mm_bank_free[b] = (sname, val)
                    ready.append((sname, val))
                return ready

            def evac_q(oc, b, pev):
                R.wait("scalar", s_pe, pev)
                v = R.op("scalar", lambda e, oc=oc, b=b: e.activation(
                    q_sb[:, oc * T:(oc + 1) * T], pF[b][:, :], AF.Copy, scale=0.125),
                    inc=s_act)
                return ("s_act", v)

            def evac_k(oc, b, pev):
                R.wait("vector", s_pe, pev)
                v = R.op("vector", lambda e, oc=oc, b=b: e.tensor_copy(
                    k_cur[:, oc * T:(oc + 1) * T], pF[b][:, :]), inc=s_dve)
                return ("s_dve", v)

            def evac_v(oc, b, pev):
                R.wait("vector", s_pe, pev)
                v = R.op("vector", lambda e, oc=oc, b=b: e.tensor_copy(
                    af_sb[:, oc * T:(oc + 1) * T], pF[b][:, :]), inc=s_dve)
                return ("s_dve", v)

            if tile == 0:
                R.wait("sync", s_wst, transform_stores_all)
                R.wait("sync", s_cw["q"], cast_done["q"])
            q_ready = proj(sq_d, evac_q)
            if tile == 0:
                R.wait("sync", s_cw["k"], cast_done["k"])
            k_ready = proj(sk_d, evac_k)
            if tile == 0:
                R.wait("gpsimd", s_pe, s_pe.n)
                for w in defer_13:
                    emit_casts(w)
            if tile == 0:
                R.wait("sync", s_cw["v"], cast_done["v"])
            v_ready = proj(sv_d, evac_v)
            if n_phases <= 2:
                dbg_stores()
                continue

            vt_done = {}
            vt_evac = {}
            ti = 0
            for oc in range(DC):
                R.wait("tensor", s_dve, v_ready[oc][1])
                for tb in range(NB):
                    b = ti % 2
                    if ti >= 2:
                        R.wait("tensor", s_dve, vt_evac[ti - 2])
                    vt_done[ti] = R.op("tensor", lambda e, b=b, oc=oc, tb=tb: e.transpose(
                        pT[b][:, 0:128], af_sb[:, oc * T + tb * 128: oc * T + (tb + 1) * 128],
                        ident[:, :]), inc=s_pe)
                    R.wait("vector", s_pe, vt_done[ti])
                    vt_evac[ti] = R.op("vector", lambda e, b=b, oc=oc, tb=tb: e.tensor_copy(
                        v_cur[:, tb * D + oc * 128: tb * D + (oc + 1) * 128], pT[b][:, 0:128]),
                        inc=s_dve)
                    ti += 1
            vt_all = vt_evac[ti - 1]
            if n_phases <= 3:
                dbg_stores()
                continue

            SCB = (4, 5, 3)
            AVB = (2, 0)
            peS, peT_, peP = {}, {}, {}
            dveM, dveE = {}, {}
            actE, actA, actTS = {}, {}, {}
            q_all = q_ready[DC - 1][1]
            k_all = k_ready[DC - 1][1]

            def hb_params(j):
                h, i = j // NB, j % NB
                oc, po = h // HPC, (h % HPC) * DK
                return i, h, oc, po

            def emit_SC(j):
                i, h, oc, po = hb_params(j)
                b = SCB[j % 3]
                if j >= 3:
                    R.wait("tensor", s_act, actE[j - 3])
                else:
                    bank_wait("tensor", b)
                if j == 0:
                    R.wait("tensor", s_act, q_all)
                    R.wait("tensor", s_dve, k_all)
                    R.wait("tensor", s_pool, carry_pool[0])
                msel = m0 if (seq_start and i == 0) else mG
                qs = q_sb[po:po + DK, oc * T + i * 128: oc * T + (i + 1) * 128]
                if i == 0:
                    kp = k_prev[po:po + DK, oc * 128:(oc + 1) * 128]
                    kc = k_cur[po:po + DK, oc * T: oc * T + 128]
                    R.op("tensor", lambda e, b=b, qs=qs, kp=kp: e.matmul(
                        pF[b][:, 0:128], qs, kp, start=True, stop=False))
                    R.op("tensor", lambda e, b=b, qs=qs, kc=kc: e.matmul(
                        pF[b][:, 128:256], qs, kc, start=False, stop=False))
                else:
                    kw = k_cur[po:po + DK, oc * T + (i - 1) * 128: oc * T + (i + 1) * 128]
                    R.op("tensor", lambda e, b=b, qs=qs, kw=kw: e.matmul(
                        pF[b][:, 0:256], qs, kw, start=True, stop=False))
                peS[j] = R.op("tensor", lambda e, b=b, msel=msel: e.matmul(
                    pF[b][:, 0:256], ident[:, :], msel[:, :], start=False, stop=True),
                    inc=s_pe)

            def emit_TR(j):
                b = j % 2
                R.wait("tensor", s_act, actTS[j])
                if j >= 2:
                    R.wait("tensor", s_dve, dveE[j - 2])
                R.op("tensor", lambda e, b=b: e.transpose(
                    pT[b][:, 0:128], pb_sb[b][:, 0:128], ident[:, :]))
                peT_[j] = R.op("tensor", lambda e, b=b: e.transpose(
                    pT[b][:, 128:256], pb_sb[b][:, 128:256], ident[:, :]), inc=s_pe)

            def emit_PV(j):
                i, h, oc, po = hb_params(j)
                ab = AVB[j % 2]
                R.wait("tensor", s_dve, dveE[j])
                if j == 0:
                    R.wait("tensor", s_dve, vt_all)
                    R.wait("tensor", s_pool, carry_pool[1])
                if j >= 2:
                    R.wait("tensor", s_act, actA[j - 2])
                else:
                    bank_wait("tensor", ab)
                v0 = (v_prev[:, h * DK: h * DK + DK] if i == 0
                      else v_cur[:, (i - 1) * D + h * DK: (i - 1) * D + h * DK + DK])
                v1 = v_cur[:, i * D + h * DK: i * D + h * DK + DK]
                R.op("tensor", lambda e, ab=ab, v0=v0, b=j % 2: e.matmul(
                    pF[ab][0:DK, 0:128], v0, ptT[b][:, 0:128], start=True, stop=False))
                peP[j] = R.op("tensor", lambda e, ab=ab, v1=v1, b=j % 2: e.matmul(
                    pF[ab][0:DK, 0:128], v1, ptT[b][:, 128:256], start=False, stop=True),
                    inc=s_pe)

            def emit_MAX(j):
                b = j % 3
                R.wait("vector", s_pe, peS[j])
                if j >= 3:
                    R.wait("vector", s_act, actE[j - 3])
                dveM[j] = R.op("vector", lambda e, b=b: e.reduce_max(
                    nmx[b][:, :], pF[SCB[b]][:, 0:256], axis=AX.X, negate=True), inc=s_dve)

            dveR = {}

            def emit_RECIP(j):
                b = j % 3
                R.wait("vector", s_act, actE[j])
                if j >= 3:
                    R.wait("vector", s_act, actTS[j - 3])
                dveR[j] = R.op("vector", lambda e, b=b: e.reciprocal(
                    rsr[b][:, :], smr[b][:, :]), inc=s_dve)

            def emit_TSc(j):
                b = j % 3
                pb = j % 2
                R.wait("scalar", s_dve, dveR[j])
                if j >= 2:
                    R.wait("scalar", s_pe, peT_[j - 2])
                actTS[j] = R.op("scalar", lambda e, b=b, pb=pb: e.activation(
                    pb_sb[pb][:, :], p_sb[b][:, :], AF.Copy, scale=rsr[b][:, 0:1]),
                    inc=s_act)

            def emit_PTE(j):
                b = j % 2
                R.wait("vector", s_pe, peT_[j])
                if j >= 2:
                    R.wait("vector", s_pe, peP[j - 2])
                dveE[j] = R.op("vector", lambda e, b=b: e.tensor_copy(
                    ptT[b][:, :], pT[b][:, :]), inc=s_dve)

            def emit_EXP(j):
                b = j % 3
                R.wait("scalar", s_dve, dveM[j])
                if j >= 3:
                    R.wait("scalar", s_act, actTS[j - 3], force=True)
                actE[j] = R.op("scalar", lambda e, b=b: e.activation(
                    p_sb[b][:, :], pF[SCB[b]][:, 0:256], AF.Exp, bias=nmx[b][:, 0:1],
                    accum_out=smr[b][:, 0:1]), inc=s_act)

            def emit_AVE(j):
                i, h, oc, po = hb_params(j)
                ab = AVB[j % 2]
                R.wait("scalar", s_pe, peP[j])
                actA[j] = R.op("scalar", lambda e, ab=ab, oc=oc, po=po, i=i: e.activation(
                    af_sb[po:po + DK, oc * T + i * 128: oc * T + (i + 1) * 128],
                    pF[ab][0:DK, 0:128], AF.Copy), inc=s_act)

            for j in range(NHB + 6):
                if j < NHB:
                    emit_SC(j)
                if 0 <= j - 3 < NHB:
                    emit_PTE(j - 3)
                if 0 <= j - 5 < NHB:
                    emit_AVE(j - 5)
                if 0 <= j - 1 < NHB:
                    emit_RECIP(j - 1)
                if j < NHB:
                    emit_MAX(j)
                if 0 <= j - 1 < NHB:
                    emit_TSc(j - 1)
                if j < NHB:
                    emit_EXP(j)
                if 0 <= j - 2 < NHB:
                    emit_TR(j - 2)
                if 0 <= j - 4 < NHB:
                    emit_PV(j - 4)

            for k in range(3):
                mm_bank_free[SCB[(NHB - 1 - k) % 3]] = ("s_act", actE[NHB - 1 - k])
            for k in range(2):
                mm_bank_free[AVB[(NHB - 1 - k) % 2]] = ("s_act", actA[NHB - 1 - k])
            attn_all = actA[NHB - 1]

            if (tile + 1) % TPS != 0:
                R.wait("gpsimd", s_pe, peS[NHB - 1])
                kc3 = k_cur[:, :].rearrange("p (oc t) -> p oc t", t=T)
                kp3 = k_prev[:, :].rearrange("p (oc t) -> p oc t", t=128)
                carry_pool[0] = R.op("gpsimd", lambda e, kc3=kc3, kp3=kp3: e.tensor_copy(
                    kp3[:, :, :], kc3[:, :, T - 128:T]), inc=s_pool)
                R.wait("gpsimd", s_pe, peP[NHB - 1])
                carry_pool[1] = R.op("gpsimd", lambda e: e.tensor_copy(
                    v_prev[:, :], v_cur[:, (NB - 1) * D: NB * D]), inc=s_pool)
            if n_phases <= 4:
                dbg_stores()
                continue

            wo_evac = {}
            if tile == 0:
                R.wait("sync", s_cw["o"], cast_done["o"])
            for oc in range(DC):
                seq, slot, wv_ = stream_stripe(so_d[oc])
                b = oc % 2
                R.wait("tensor", wv_[0], wv_[1])
                bank_wait("tensor", b)
                if oc == 0:
                    R.wait("tensor", s_act, attn_all)
                for dc2 in range(DC):
                    pev = R.op("tensor", lambda e, b=b, slot=slot, dc2=dc2: e.matmul(
                        pF[b][:, :], slot[:, dc2 * 128:(dc2 + 1) * 128],
                        af_sb[:, dc2 * T:(dc2 + 1) * T],
                        start=(dc2 == 0), stop=(dc2 == DC - 1)), inc=s_pe)
                mark_consumed(seq, pev)
                R.wait("vector", s_pe, pev)
                wo_evac[oc] = R.op("vector", lambda e, oc=oc, b=b: e.tensor_tensor(
                    x_sb[:, oc * T:(oc + 1) * T], x_sb[:, oc * T:(oc + 1) * T],
                    pF[b][:, :], ALU.add), inc=s_dve)
                mm_bank_free[b] = ("s_dve", wo_evac[oc])
            if n_phases <= 5:
                dbg_stores()
                continue

            hid_done = {}
            silu_done = {}
            if tile == 0:
                R.wait("sync", s_cw["1"], cast_done["1"])
                R.wait("sync", s_cw["3"], cast_done["3"])
            for hc in range(HC):
                gb, lb = ((4, 5), (2, 3))[hc % 2]
                seq1, slot1, wv1 = stream_stripe(s1_d[hc])
                seq3, slot3, wv3 = stream_stripe(s3_d[hc])
                if hc == DC:
                    R.wait("vector", s_pool, carry_pool[0])
                if hc == 3 * DC:
                    R.wait("vector", s_pool, carry_pool[1])
                R.wait("tensor", wv1[0], wv1[1])
                bank_wait("tensor", gb)
                for dc2 in range(DC):
                    peg = R.op("tensor", lambda e, gb=gb, slot1=slot1, dc2=dc2: e.matmul(
                        pF[gb][:, :], slot1[:, dc2 * 128:(dc2 + 1) * 128],
                        xn_sb[:, dc2 * T:(dc2 + 1) * T],
                        start=(dc2 == 0), stop=(dc2 == DC - 1)), inc=s_pe)
                mark_consumed(seq1, peg)
                R.wait("tensor", wv3[0], wv3[1])
                bank_wait("tensor", lb)
                for dc2 in range(DC):
                    pel = R.op("tensor", lambda e, lb=lb, slot3=slot3, dc2=dc2: e.matmul(
                        pF[lb][:, :], slot3[:, dc2 * 128:(dc2 + 1) * 128],
                        xn_sb[:, dc2 * T:(dc2 + 1) * T],
                        start=(dc2 == 0), stop=(dc2 == DC - 1)), inc=s_pe)
                mark_consumed(seq3, pel)
                R.wait("scalar", s_pe, peg)
                if hc >= 2:
                    R.wait("scalar", s_dve, hid_done[hc - 2])
                silu_done[hc] = R.op("scalar", lambda e, hc=hc, gb=gb: e.activation(
                    sg[hc % 2][:, :], pF[gb][:, :], AF.Silu), inc=s_act)
                mm_bank_free[gb] = ("s_act", silu_done[hc])
                R.wait("vector", s_pe, pel)
                R.wait("vector", s_act, silu_done[hc])
                hid_done[hc] = R.op("vector", lambda e, hc=hc, lb=lb: e.tensor_tensor(
                    hidden[:, hc * T:(hc + 1) * T], sg[hc % 2][:, :], pF[lb][:, :],
                    ALU.mult), inc=s_dve)
                mm_bank_free[lb] = ("s_dve", hid_done[hc])
            if n_phases <= 6:
                dbg_stores()
                continue

            if tile == 0:
                R.wait("sync", s_wst, transform_stores_all)
            for dc2 in range(DC):
                if w2_pe_prev[dc2 % 2] is not None:
                    R.wait("sync", s_pe, w2_pe_prev[dc2 % 2])
                wv_ = R.op("sync", lambda e, dc2=dc2: e.dma_start(
                    w2buf[dc2 % 2][:, :].rearrange("p (a b) -> p a b", b=128), s2_d[dc2]),
                    inc=s_w2b[dc2 % 2], inc_by=16)
                b = 4 + (dc2 % 2)
                R.wait("tensor", s_w2b[dc2 % 2], wv_)
                bank_wait("tensor", b)
                R.wait("tensor", s_dve, hid_done[HC - 1])
                for hc in range(HC):
                    pev = R.op("tensor", lambda e, b=b, dc2=dc2, hc=hc: e.matmul(
                        pF[b][:, :], w2buf[dc2 % 2][:, hc * 128:(hc + 1) * 128],
                        hidden[:, hc * T:(hc + 1) * T],
                        start=(hc == 0), stop=(hc == HC - 1)), inc=s_pe)
                w2_pe_prev[dc2 % 2] = pev
                R.wait("vector", s_pe, pev)
                ev = R.op("vector", lambda e, dc2=dc2, b=b: e.tensor_tensor(
                    x_sb[:, dc2 * T:(dc2 + 1) * T], x_sb[:, dc2 * T:(dc2 + 1) * T],
                    pF[b][:, :], ALU.add), inc=s_dve)
                mm_bank_free[b] = ("s_dve", ev)
                R.wait("gpsimd", s_dve, ev)
                R.op("gpsimd", lambda e, dc2=dc2, t0=t0: e.dma_start(
                    out_d[dc2][:, t0:t0 + T], x_sb[:, dc2 * T:(dc2 + 1) * T]),
                    inc=s_sc[dc2], inc_by=16)

        for dc2 in range(DC):
            R.wait("gpsimd", s_sc[dc2], s_sc[dc2].n)

        with nc.Block() as block:
            @block.tensor
            def _(t):
                for f in R.prog["tensor"]:
                    f(t)

            @block.vector
            def _(v):
                for f in R.prog["vector"]:
                    f(v)

            @block.scalar
            def _(s):
                for f in R.prog["scalar"]:
                    f(s)

            @block.sync
            def _(sy):
                for f in R.prog["sync"]:
                    f(sy)

            @block.gpsimd
            def _(g):
                for f in R.prog["gpsimd"]:
                    f(g)

    return nc


_CACHED = {}


def _get_program():
    if "nc" not in _CACHED:
        _CACHED["nc"] = build_program()
    return _CACHED["nc"]


def kernel(L0_attn_x, L0_attn_Wq, L0_attn_Wk, L0_attn_Wv, L0_attn_Wo,
           L0_ffn_W1, L0_ffn_W2, L0_ffn_W3):
    B, S, D = L0_attn_x.shape
    HF = L0_ffn_W1.shape[0]
    DC, HC = D // 128, HF // 128
    n_cores = 8
    bpc = B // n_cores

    def lhsT_layout(W, n_out):
        return np.ascontiguousarray(
            np.asarray(W).reshape(n_out, 128, DC, 128).transpose(0, 3, 2, 1))

    wq = lhsT_layout(L0_attn_Wq, DC)
    wk = lhsT_layout(L0_attn_Wk, DC)
    wv = lhsT_layout(L0_attn_Wv, DC)
    wo = lhsT_layout(L0_attn_Wo, DC)
    w1 = lhsT_layout(L0_ffn_W1, HC)
    w3 = lhsT_layout(L0_ffn_W3, HC)
    w2 = np.ascontiguousarray(
        np.asarray(L0_ffn_W2).reshape(DC, 128, HC, 128).transpose(0, 3, 2, 1))

    x = np.asarray(L0_attn_x)
    in_maps = []
    for core in range(n_cores):
        xc = np.ascontiguousarray(
            x[core * bpc:(core + 1) * bpc].transpose(2, 0, 1).reshape(D, bpc * S)
            .reshape(DC, 128, bpc * S))
        in_maps.append({"x": xc, "wq": wq, "wk": wk, "wv": wv, "wo": wo,
                        "w1": w1, "w3": w3, "w2": w2})

    nc = _get_program()
    res = run_bass_kernel_spmd(nc, in_maps, core_ids=list(range(n_cores)))

    out = np.empty((B, S, D), dtype=np.float32)
    for core in range(n_cores):
        oc_ = np.asarray(res.results[core]["out"]).reshape(D, bpc, S)
        out[core * bpc:(core + 1) * bpc] = oc_.transpose(1, 2, 0)
    return out


if __name__ == "__main__":
    import json
    rng = np.random.default_rng(0)
    print("building program...")
    nc = _get_program()
    print("n_instructions:", nc.n_instructions())


# version 11
# speedup vs baseline: 1.1098x; 1.0947x over previous
import numpy as np
import ml_dtypes
from contextlib import ExitStack

import concourse.bass as bass
import concourse.mybir as mybir
from concourse.bass_utils import run_bass_kernel_spmd

F32 = mybir.dt.float32
BF16 = mybir.dt.bfloat16
AF = mybir.ActivationFunctionType
ALU = mybir.AluOpType
AX = mybir.AxisListType

ENGS = ("tensor", "vector", "scalar", "sync", "gpsimd")


def hadamard(n):
    H = np.array([[1.0]])
    while H.shape[0] < n:
        H = np.block([[H, H], [H, -H]])
    return H


class Sem:

    def __init__(self, handle, name):
        self.h = handle
        self.name = name
        self.n = 0


class Rec:

    def __init__(self):
        self.prog = {e: [] for e in ENGS}
        self.mark = {e: {} for e in ENGS}

    def wait(self, eng, sem, val, force=False):
        if val <= 0:
            return
        if not force and self.mark[eng].get(sem.name, -1) >= val:
            return
        if self.mark[eng].get(sem.name, -1) < val:
            self.mark[eng][sem.name] = val
        self.prog[eng].append(lambda e, s=sem.h, v=val: e.wait_ge(s, v))

    def op(self, eng, fn, inc=None, inc_by=1):
        if inc is None:
            self.prog[eng].append(fn)
            return None
        inc.n += inc_by
        v = inc.n
        self.prog[eng].append(lambda e, f=fn, s=inc.h, k=inc_by: f(e).then_inc(s, k))
        if inc_by == 1 and self.mark[eng].get(inc.name, -1) < v:
            self.mark[eng][inc.name] = v
        return v


def build_program(D=2048, HF=8192, T=512, NTOK=4096, SEQ=2048, HEADS=32, DK=64,
                  n_phases=7, n_cores=1):
    DC = D // 128
    HC = HF // 128
    NB = T // 128
    NT = NTOK // T
    TPS = SEQ // T
    HPC = 128 // DK
    NHB = NB * HEADS

    nc = bass.Bass("TRN2", target_bir_lowering=False)

    x_d = nc.dram_tensor("x", [DC, 128, NTOK], F32, kind="ExternalInput").ap()
    wq_d = nc.dram_tensor("wq", [DC, 128, DC, 128], F32, kind="ExternalInput").ap()
    wk_d = nc.dram_tensor("wk", [DC, 128, DC, 128], F32, kind="ExternalInput").ap()
    wv_d = nc.dram_tensor("wv", [DC, 128, DC, 128], F32, kind="ExternalInput").ap()
    wo_d = nc.dram_tensor("wo", [DC, 128, DC, 128], F32, kind="ExternalInput").ap()
    w1_d = nc.dram_tensor("w1", [HC, 128, DC, 128], F32, kind="ExternalInput").ap()
    w3_d = nc.dram_tensor("w3", [HC, 128, DC, 128], F32, kind="ExternalInput").ap()
    DCL = DC // n_cores if n_cores > 1 else DC
    assert DCL * max(n_cores, 1) == DC or n_cores == 1
    w2_d = nc.dram_tensor("w2", [DCL, 128, HC, 128], F32, kind="ExternalInput").ap()
    out_d = nc.dram_tensor("out", [DC, 128, NTOK], F32, kind="ExternalOutput").ap()

    sq_d = nc.dram_tensor("sq", [DC, 128, DC, 128], BF16, kind="Internal").ap()
    sk_d = nc.dram_tensor("sk", [DC, 128, DC, 128], BF16, kind="Internal").ap()
    sv_d = nc.dram_tensor("sv", [DC, 128, DC, 128], BF16, kind="Internal").ap()
    so_d = nc.dram_tensor("so", [DC, 128, DC, 128], BF16, kind="Internal").ap()
    s1_d = nc.dram_tensor("s1", [HC, 128, DC, 128], BF16, kind="Internal").ap()
    s3_d = nc.dram_tensor("s3", [HC, 128, DC, 128], BF16, kind="Internal").ap()
    s2_d = nc.dram_tensor("s2", [DC, 128, HC, 128], BF16, kind="Internal").ap()
    s2l_d = (nc.dram_tensor("s2l", [DCL, 128, HC, 128], BF16, kind="Internal").ap()
             if n_cores > 1 else s2_d)

    ident_np = np.eye(128, dtype=ml_dtypes.bfloat16)
    Hb_np = hadamard(128).astype(ml_dtypes.bfloat16)
    r = np.arange(128)[:, None]
    c = np.arange(256)[None, :]
    validG = (c >= r) & (c <= r + 128)
    mG_np = np.where(validG, 0.0, -1e9).astype(ml_dtypes.bfloat16)
    m0_np = np.where(validG & (c >= 128), 0.0, -1e9).astype(ml_dtypes.bfloat16)
    onesn_np = np.full((128, 1), -1.0 / D, dtype=np.float32)
    onesp_np = np.full((128, 1), 1.0 / D, dtype=ml_dtypes.bfloat16)
    ones1_np = np.ones((1, 128), dtype=np.float32)
    eps_np = np.full((1, 1), 1e-5, dtype=np.float32)

    ident_d = nc.inline_tensor(ident_np, name="identC").ap()
    Hb_d = nc.inline_tensor(Hb_np, name="HbC").ap()
    mG_d = nc.inline_tensor(mG_np, name="mGC").ap()
    m0_d = nc.inline_tensor(m0_np, name="m0C").ap()
    onesn_d = nc.inline_tensor(onesn_np, name="onesnC").ap()
    onesp_d = nc.inline_tensor(onesp_np, name="onespC").ap()
    ones1_d = nc.inline_tensor(ones1_np, name="ones1C").ap()
    eps_d = nc.inline_tensor(eps_np, name="epsC").ap()

    R = Rec()

    with ExitStack() as ctx:
        E = ctx.enter_context

        x_sb = E(nc.sbuf_tensor("x_sb", [128, DC * T], F32))
        xn_sb = E(nc.sbuf_tensor("xn_sb", [128, DC * T], BF16))
        k_prev = E(nc.sbuf_tensor("k_prev", [128, DC * 128], BF16))
        v_prev = E(nc.sbuf_tensor("v_prev", [128, D], BF16))
        ident = E(nc.sbuf_tensor("ident", [128, 128], BF16))
        Hb = E(nc.sbuf_tensor("Hb", [128, 128], BF16))
        mm_bank_free = {}
        mG = E(nc.sbuf_tensor("mG", [128, 256], BF16))
        m0 = E(nc.sbuf_tensor("m0", [128, 256], BF16))
        onesn = E(nc.sbuf_tensor("onesn", [128, 1], F32))
        onesp = E(nc.sbuf_tensor("onesp", [128, 1], BF16))
        ones1 = E(nc.sbuf_tensor("ones1", [1, 128], F32))
        eps_sb = E(nc.sbuf_tensor("eps_sb", [1, 1], F32))
        meanrow = E(nc.sbuf_tensor("meanrow", [1, T], F32))
        m2row = E(nc.sbuf_tensor("m2row", [1, T], F32))
        varrow = E(nc.sbuf_tensor("varrow", [1, T], F32))
        lnrow = E(nc.sbuf_tensor("lnrow", [1, T], F32))
        rstdrow = E(nc.sbuf_tensor("rstdrow", [1, T], F32))
        sqb = [E(nc.sbuf_tensor(f"sqb{i}", [128, T], BF16)) for i in range(2)]
        xctmp = E(nc.sbuf_tensor("xctmp", [128, T], F32))
        nmx = [E(nc.sbuf_tensor(f"nmx{i}", [128, 1], F32)) for i in range(3)]
        smr = [E(nc.sbuf_tensor(f"smr{i}", [128, 1], F32)) for i in range(3)]
        rsr = [E(nc.sbuf_tensor(f"rsr{i}", [128, 1], F32)) for i in range(3)]
        p_sb = [E(nc.sbuf_tensor(f"p_sb{i}", [128, 256], F32)) for i in range(3)]
        pb_sb = [E(nc.sbuf_tensor(f"pb_sb{i}", [128, 256], BF16)) for i in range(2)]
        ptT = [E(nc.sbuf_tensor(f"ptT{i}", [128, 256], BF16)) for i in range(2)]
        sg = [E(nc.sbuf_tensor(f"sg{i}", [128, T], F32)) for i in range(2)]

        sems = {}
        for name in ("s_pe", "s_dve", "s_act", "s_pool", "s_wst", "s_wld",
                     "s_x", "s_st", "s_cvt"):
            sems[name] = Sem(E(nc.semaphore(name)), name)
        s_pe, s_dve, s_act, s_pool = (sems[k] for k in ("s_pe", "s_dve", "s_act", "s_pool"))
        s_wst, s_wld, s_x, s_st, s_cvt = (
            sems[k] for k in ("s_wst", "s_wld", "s_x", "s_st", "s_cvt"))
        NRING = 6
        s_rg = []
        for i in range(NRING):
            s_rg.append(Sem(E(nc.semaphore(f"s_rg{i}")), f"s_rg{i}"))
            sems[f"s_rg{i}"] = s_rg[i]
        s_w2b = []
        for i in range(2):
            s_w2b.append(Sem(E(nc.semaphore(f"s_w2b{i}")), f"s_w2b{i}"))
            sems[f"s_w2b{i}"] = s_w2b[i]
        s_xc = []
        s_sc = []
        for i in range(DC):
            s_xc.append(Sem(E(nc.semaphore(f"s_xc{i}")), f"s_xc{i}"))
            sems[f"s_xc{i}"] = s_xc[i]
            s_sc.append(Sem(E(nc.semaphore(f"s_sc{i}")), f"s_sc{i}"))
            sems[f"s_sc{i}"] = s_sc[i]

        s_cst = Sem(E(nc.semaphore("s_cst")), "s_cst")
        sems["s_cst"] = s_cst
        s_cw = {}
        for w in ("q", "k", "v", "o", "1", "3"):
            s_cw[w] = Sem(E(nc.semaphore(f"s_cw{w}")), f"s_cw{w}")
            sems[f"s_cw{w}"] = s_cw[w]
        s_cc = Sem(E(nc.semaphore("s_cc")), "s_cc")
        sems["s_cc"] = s_cc
        s_os = []
        for i in range(2):
            s_os.append(Sem(E(nc.semaphore(f"s_os{i}")), f"s_os{i}"))
            sems[f"s_os{i}"] = s_os[i]
        s_wl3 = []
        for i in range(4):
            s_wl3.append(Sem(E(nc.semaphore(f"s_wl3{i}")), f"s_wl3{i}"))
            sems[f"s_wl3{i}"] = s_wl3[i]

        for dst, src in ((ident, ident_d), (Hb, Hb_d), (mG, mG_d), (m0, m0_d),
                         (onesn, onesn_d), (onesp, onesp_d), (ones1, ones1_d),
                         (eps_sb, eps_d)):
            R.op("sync", lambda e, d=dst, s=src: e.dma_start(d[:, :], s), inc=s_cst, inc_by=16)
        cst_all = s_cst.n

        cast_specs = {"q": (wq_d, sq_d, DC), "k": (wk_d, sk_d, DC),
                      "v": (wv_d, sv_d, DC), "o": (wo_d, so_d, DC),
                      "1": (w1_d, s1_d, HC), "3": (w3_d, s3_d, HC)}
        cast_done = {}

        def emit_casts(name):
            src, dst, n = cast_specs[name]
            for i in range(n):
                R.op("gpsimd", lambda e, s=src, d=dst, i=i: e.dma_start(
                    d[i].rearrange("p a b -> p (a b)"), s[i].rearrange("p a b -> p (a b)")),
                    inc=s_cw[name], inc_by=16)
            cast_done[name] = s_cw[name].n

        with nc.sbuf_tensor("slabX0", [128, HC * 128], BF16) as X0, \
             nc.sbuf_tensor("slabX1", [128, HC * 128], BF16) as X1, \
             nc.sbuf_tensor("slabX2", [128, HC * 128], BF16) as X2, \
             nc.sbuf_tensor("slabX3", [128, HC * 128], BF16) as X3, \
             nc.sbuf_tensor("slabS", [128, HC * 128], BF16) as Ssl, \
             nc.sbuf_tensor("outsl0", [128, HC * 128], BF16) as O0, \
             nc.sbuf_tensor("outsl1", [128, HC * 128], BF16) as O1, \
             nc.psum_tensor("pW0", [128, 512], F32) as pW0, \
             nc.psum_tensor("pW1", [128, 512], F32) as pW1:
            X = [X0, X1, X2, X3]
            OSL = [O0, O1]
            pW = [pW0, pW1]
            n_stage = int(np.log2(HC))
            mm_last = {}
            evac_last = {}
            store_done = {}

            def emit_slab_load(sl):
                if sl >= DCL:
                    return
                if sl >= 4:
                    R.wait("gpsimd", s_pe, mm_last[sl - 4])
                R.op("gpsimd", lambda e, sl=sl: e.dma_start(
                    X[sl % 4][:, :], w2_d[sl].rearrange("p a b -> p (a b)")),
                    inc=s_wl3[sl % 4], inc_by=16)

            evac_g = {}
            for sl in range(min(4, DCL)):
                emit_slab_load(sl)
            emit_casts("q")
            emit_casts("k")
            cast_after = {0: "v", 2: "o"} if DCL > 2 else {}

            fin_was_S = False
            prev_bfly = None
            for sl in range(DCL):
                R.wait("vector", s_wl3[sl % 4], 16 * (sl // 4 + 1))
                if fin_was_S and sl > 0:
                    R.wait("vector", s_pe, mm_last[sl - 1])
                if prev_bfly is not None:
                    R.wait("vector", s_dve, prev_bfly, force=True)
                src, dst = X[sl % 4], Ssl
                vlast = None
                for s in range(n_stage):
                    span = (2 ** s) * 128
                    a_s = src[:, :].rearrange("p (g two x) -> p g two x", two=2, x=span)
                    a_d = dst[:, :].rearrange("p (g two x) -> p g two x", two=2, x=span)
                    if vlast is not None:
                        R.wait("vector", s_dve, vlast, force=True)
                    R.op("vector", lambda e, a_d=a_d, a_s=a_s: e.tensor_tensor(
                        a_d[:, :, 0, :], a_s[:, :, 0, :], a_s[:, :, 1, :], ALU.add),
                        inc=s_dve)
                    vlast = R.op("vector", lambda e, a_d=a_d, a_s=a_s: e.tensor_tensor(
                        a_d[:, :, 1, :], a_s[:, :, 0, :], a_s[:, :, 1, :], ALU.subtract),
                        inc=s_dve)
                    src, dst = dst, src
                bfly_done = vlast
                prev_bfly = vlast
                fin = src
                fin_was_S = fin is Ssl
                R.wait("tensor", s_dve, bfly_done)
                if sl == 0:
                    R.wait("tensor", s_cst, cst_all)
                for g in range(HC // 4):
                    b = g % 2
                    if sl * (HC // 4) + g >= 2:
                        R.wait("tensor", s_act, evac_g[(sl * (HC // 4) + g) - 2])
                    for cc in range(4):
                        cch = g * 4 + cc
                        mm_last[sl] = R.op("tensor", lambda e, b=b, fin=fin, cch=cch, cc=cc: e.matmul(
                            pW[b][:, cc * 128:(cc + 1) * 128], Hb[:, :],
                            fin[:, cch * 128:(cch + 1) * 128],
                            start=(cc == 0), stop=(cc == 3)), inc=s_pe)
                    if g == 0 and sl >= 2:
                        R.wait("scalar", s_os[sl % 2], store_done[sl - 2])
                    R.wait("scalar", s_pe, mm_last[sl])
                    evac_g[sl * (HC // 4) + g] = R.op(
                        "scalar", lambda e, b=b, g=g, sl=sl: e.activation(
                            OSL[sl % 2][:, g * 512:(g + 1) * 512], pW[b][:, :],
                            AF.Copy, scale=float(1.0 / np.sqrt(HF))), inc=s_act)
                evac_last[sl] = evac_g[sl * (HC // 4) + (HC // 4) - 1]
                R.wait("scalar", s_act, evac_last[sl], force=True)
                store_done[sl] = R.op("scalar", lambda e, sl=sl: e.dma_start(
                    s2l_d[sl].rearrange("p a b -> p (a b)"), OSL[sl % 2][:, :]),
                    inc=s_os[sl % 2], inc_by=16)
                emit_slab_load(sl + 4)
                if sl in cast_after:
                    emit_casts(cast_after[sl])

            for w in ("q", "k", "v", "o"):
                if w not in cast_done:
                    emit_casts(w)
            defer_13 = [w for w in ("1", "3") if w not in cast_done]

            transform_stores_all = (s_os[0].n, s_os[1].n)
            mm_bank_free[0] = ("s_act", evac_last[DCL - 1])
            if DCL >= 2:
                mm_bank_free[1] = ("s_act", evac_last[DCL - 2])

            if n_cores > 1:
                R.wait("gpsimd", s_os[0], transform_stores_all[0])
                R.wait("gpsimd", s_os[1], transform_stores_all[1])
                groups = [list(range(n_cores))]
                s2l_flat = s2l_d.rearrange("s p a b -> (s p) (a b)")
                s2_flat = s2_d.rearrange("s p a b -> (s p) (a b)")
                R.op("gpsimd", lambda e: e.collective_compute(
                    "AllGather", ALU.bypass, groups, [s2l_flat], [s2_flat]),
                    inc=s_cc, inc_by=16)

        alias = E(nc.sbuf_tensor("alias", [128, 4 * DC * T], BF16))
        q_sb = alias[:, 0 * DC * T:1 * DC * T]
        k_cur = alias[:, 1 * DC * T:2 * DC * T]
        af_sb = alias[:, 2 * DC * T:3 * DC * T]
        v_cur = alias[:, 3 * DC * T:4 * DC * T]
        hidden = alias[:, :]
        assert HC * T == 4 * DC * T
        ring = [E(nc.sbuf_tensor(f"ring{i}", [128, DC * 128], BF16)) for i in range(NRING)]
        w2buf = [E(nc.sbuf_tensor(f"w2buf{i}", [128, HC * 128], BF16)) for i in range(2)]

        pF = [E(nc.psum_tensor(f"pF{i}", [128, T], F32)) for i in range(6)]
        pT = [E(nc.psum_tensor(f"pT{i}", [128, 256], BF16)) for i in range(2)]


        ring_i = [0]
        ring_consumed = {}

        def stream_stripe(src_ap):
            seq = ring_i[0]
            ring_i[0] += 1
            slot = ring[seq % NRING]
            sem = s_rg[seq % NRING]
            if seq - NRING in ring_consumed:
                R.wait("sync", s_pe, ring_consumed[seq - NRING])
            v = R.op("sync", lambda e, slot=slot, s=src_ap: e.dma_start(
                slot[:, :].rearrange("p (a b) -> p a b", b=128), s), inc=sem, inc_by=16)
            return seq, slot, (sem, v)

        def mark_consumed(seq, pe_val):
            ring_consumed[seq] = pe_val

        store_v = {}

        def bank_wait(eng, bank):
            if bank in mm_bank_free:
                sname, v = mm_bank_free[bank]
                R.wait(eng, sems[sname], v)

        last_xn = [0]
        carry_pool = [0, 0]
        w2_pe_prev = {0: None, 1: None}

        ln_state = {}
        mean_pe = {}
        sq_pe_g = {}
        x_loads_pre = set()

        def emit_x_load(tile, dc2):
            R.wait("gpsimd", s_sc[dc2], 16 * tile)
            R.op("gpsimd", lambda e, dc2=dc2, t0=tile * T: e.dma_start(
                x_sb[:, dc2 * T:(dc2 + 1) * T], x_d[dc2][:, t0:t0 + T]),
                inc=s_xc[dc2], inc_by=16)

        def x_gate(eng, tile, dc2):
            if tile == 0:
                R.wait(eng, s_x, 16 * DC)
            else:
                R.wait(eng, s_xc[dc2], 16 * tile)

        def emit_mean_chunk(tile, dc2):
            if dc2 == 0:
                bank_wait("tensor", 0)
            x_gate("tensor", tile, dc2)
            mean_pe[(tile, dc2)] = R.op("tensor", lambda e, dc2=dc2: e.matmul(
                pF[0][0:1, :], onesn[:, :], x_sb[:, dc2 * T:(dc2 + 1) * T],
                start=(dc2 == 0), stop=(dc2 == DC - 1)), inc=s_pe)

        def emit_sq_chunk(tile, dc2):
            if dc2 == 0:
                bank_wait("tensor", 1)
            x_gate("scalar", tile, dc2)
            if dc2 >= 2:
                R.wait("scalar", s_pe, sq_pe_g[(tile, dc2 - 2)])
            aqv = R.op("scalar", lambda e, dc2=dc2: e.activation(
                sqb[dc2 % 2][:, :], x_sb[:, dc2 * T:(dc2 + 1) * T], AF.Square),
                inc=s_act)
            R.wait("tensor", s_act, aqv)
            sq_pe_g[(tile, dc2)] = R.op("tensor", lambda e, dc2=dc2: e.matmul(
                pF[1][0:1, :], onesp[:, :], sqb[dc2 % 2][:, :],
                start=(dc2 == 0), stop=(dc2 == DC - 1)), inc=s_pe)

        def emit_LN_rest(tile):
            R.wait("vector", s_pe, mean_pe[(tile, DC - 1)])
            v_mr = R.op("vector", lambda e: e.tensor_copy(
                meanrow[:, :], pF[0][0:1, :]), inc=s_dve)
            R.wait("scalar", s_dve, v_mr)
            a_m2 = R.op("scalar", lambda e: e.activation(
                m2row[:, :], meanrow[:, :], AF.Square), inc=s_act)
            R.wait("vector", s_pe, sq_pe_g[(tile, DC - 1)])
            R.wait("vector", s_act, a_m2)
            v_var = R.op("vector", lambda e: e.tensor_tensor(
                varrow[:, :], pF[1][0:1, :], m2row[:, :], ALU.subtract), inc=s_dve)
            R.wait("scalar", s_dve, v_var)
            a_ln = R.op("scalar", lambda e: e.activation(
                lnrow[:, :], varrow[:, :], AF.Ln, bias=eps_sb[0:1, 0:1]), inc=s_act)
            R.wait("scalar", s_act, a_ln, force=True)
            a_rstd = R.op("scalar", lambda e: e.activation(
                rstdrow[:, :], lnrow[:, :], AF.Exp, scale=-0.5), inc=s_act)
            bank_wait("tensor", 2)
            bank_wait("tensor", 3)
            R.wait("tensor", s_dve, v_mr)
            R.op("tensor", lambda e: e.matmul(
                pF[2][:, :], ones1[:, :], meanrow[:, :], start=True, stop=True),
                inc=s_pe)
            R.wait("tensor", s_act, a_rstd)
            pe_bc2 = R.op("tensor", lambda e: e.matmul(
                pF[3][:, :], ones1[:, :], rstdrow[:, :], start=True, stop=True),
                inc=s_pe)
            R.wait("vector", s_pe, pe_bc2)
            xn_done = {}
            for dc2 in range(DC):
                if dc2 > 0:
                    R.wait("vector", s_dve, last_xn[0], force=True)
                vadd = R.op("vector", lambda e, dc2=dc2: e.tensor_tensor(
                    xctmp[:, :], x_sb[:, dc2 * T:(dc2 + 1) * T], pF[2][:, :],
                    ALU.add), inc=s_dve)
                R.wait("vector", s_dve, vadd, force=True)
                last_xn[0] = R.op("vector", lambda e, dc2=dc2: e.tensor_tensor(
                    xn_sb[:, dc2 * T:(dc2 + 1) * T], xctmp[:, :], pF[3][:, :],
                    ALU.mult), inc=s_dve)
                xn_done[dc2] = last_xn[0]
            mm_bank_free[0] = ("s_dve", v_mr)
            mm_bank_free[1] = ("s_dve", v_var)
            mm_bank_free[2] = ("s_dve", last_xn[0])
            mm_bank_free[3] = ("s_dve", last_xn[0])
            ln_state[tile] = {"xn_done": xn_done, "v_mr": v_mr, "v_var": v_var}

        for tile in range(NT):
            t0 = tile * T
            seq_start = (tile % TPS) == 0

            if tile == 0:
                for e in ("tensor", "scalar", "vector"):
                    R.wait(e, s_cst, cst_all)
                for dc2 in range(DC):
                    R.op("sync", lambda e, dc2=dc2, t0=t0: e.dma_start(
                        x_sb[:, dc2 * T:(dc2 + 1) * T], x_d[dc2][:, t0:t0 + T]),
                        inc=s_x, inc_by=16)
            elif tile not in x_loads_pre:
                for dc2 in range(DC):
                    emit_x_load(tile, dc2)
            x_chunk = 16 * tile

            if n_phases <= 0:
                for dc2 in range(DC):
                    if tile == 0:
                        R.wait("gpsimd", s_x, 16 * DC)
                    else:
                        R.wait("gpsimd", s_xc[dc2], x_chunk)
                    R.op("gpsimd", lambda e, dc2=dc2, t0=t0: e.dma_start(
                        out_d[dc2][:, t0:t0 + T], x_sb[:, dc2 * T:(dc2 + 1) * T]),
                        inc=s_sc[dc2], inc_by=16)
                continue

            if seq_start:
                R.wait("gpsimd", s_pe, s_pe.n)
                carry_pool[0] = R.op("gpsimd", lambda e: e.memset(k_prev[:, :], 0.0), inc=s_pool)
                carry_pool[1] = R.op("gpsimd", lambda e: e.memset(v_prev[:, :], 0.0), inc=s_pool)

            def dbg_stores0(t0=t0, tile=tile, x_chunk=x_chunk):
                for sname in ("s_pe", "s_dve", "s_act", "s_pool"):
                    R.wait("gpsimd", sems[sname], sems[sname].n)
                for dc2 in range(DC):
                    if tile == 0:
                        R.wait("gpsimd", s_x, 16 * DC)
                    else:
                        R.wait("gpsimd", s_xc[dc2], x_chunk)
                    R.op("gpsimd", lambda e, dc2=dc2, t0=t0: e.dma_start(
                        out_d[dc2][:, t0:t0 + T], x_sb[:, dc2 * T:(dc2 + 1) * T]),
                        inc=s_sc[dc2], inc_by=16)

            if n_phases == 11:
                dbg_stores0()
                continue
            if tile not in ln_state:
                for dc2 in range(DC):
                    emit_mean_chunk(tile, dc2)
                for dc2 in range(DC):
                    emit_sq_chunk(tile, dc2)
                if n_phases == 12:
                    dbg_stores0()
                    continue
                emit_LN_rest(tile)
            xn_done = ln_state[tile]["xn_done"]
            if n_phases in (13, 14):
                dbg_stores0()
                continue

            def dbg_stores(t0=t0, tile=tile):
                for sname in ("s_pe", "s_dve", "s_act", "s_pool"):
                    R.wait("gpsimd", sems[sname], sems[sname].n)
                for dc2 in range(DC):
                    R.op("gpsimd", lambda e, dc2=dc2, t0=t0: e.dma_start(
                        out_d[dc2][:, t0:t0 + T], x_sb[:, dc2 * T:(dc2 + 1) * T]),
                        inc=s_sc[dc2], inc_by=16)

            if n_phases <= 1:
                dbg_stores()
                continue

            def proj(scr, evac):
                ready = []
                for oc in range(DC):
                    seq, slot, wv_ = stream_stripe(scr[oc])
                    b = 4 + (oc % 2)
                    R.wait("tensor", wv_[0], wv_[1])
                    bank_wait("tensor", b)
                    for dc2 in range(DC):
                        R.wait("tensor", s_dve, xn_done[dc2])
                        pev = R.op("tensor", lambda e, b=b, slot=slot, dc2=dc2: e.matmul(
                            pF[b][:, :], slot[:, dc2 * 128:(dc2 + 1) * 128],
                            xn_sb[:, dc2 * T:(dc2 + 1) * T],
                            start=(dc2 == 0), stop=(dc2 == DC - 1)), inc=s_pe)
                    mark_consumed(seq, pev)
                    sname, val = evac(oc, b, pev)
                    mm_bank_free[b] = (sname, val)
                    ready.append((sname, val))
                return ready

            def evac_q(oc, b, pev):
                R.wait("scalar", s_pe, pev)
                v = R.op("scalar", lambda e, oc=oc, b=b: e.activation(
                    q_sb[:, oc * T:(oc + 1) * T], pF[b][:, :], AF.Copy, scale=0.125),
                    inc=s_act)
                return ("s_act", v)

            def evac_k(oc, b, pev):
                R.wait("vector", s_pe, pev)
                v = R.op("vector", lambda e, oc=oc, b=b: e.tensor_copy(
                    k_cur[:, oc * T:(oc + 1) * T], pF[b][:, :]), inc=s_dve)
                return ("s_dve", v)

            def evac_v(oc, b, pev):
                R.wait("vector", s_pe, pev)
                v = R.op("vector", lambda e, oc=oc, b=b: e.tensor_copy(
                    af_sb[:, oc * T:(oc + 1) * T], pF[b][:, :]), inc=s_dve)
                return ("s_dve", v)

            if tile == 0:
                R.wait("sync", s_os[0], transform_stores_all[0])
                R.wait("sync", s_os[1], transform_stores_all[1])
                R.wait("sync", s_cw["q"], cast_done["q"])
            q_ready = proj(sq_d, evac_q)
            if tile == 0:
                R.wait("sync", s_cw["k"], cast_done["k"])
            k_ready = proj(sk_d, evac_k)
            if tile == 0:
                R.wait("gpsimd", s_pe, s_pe.n)
                for w in defer_13:
                    emit_casts(w)
            if tile == 0:
                R.wait("sync", s_cw["v"], cast_done["v"])
            v_ready = proj(sv_d, evac_v)
            if n_phases <= 2:
                dbg_stores()
                continue

            vt_done = {}
            vt_evac = {}
            ti = 0
            for oc in range(DC):
                R.wait("tensor", s_dve, v_ready[oc][1])
                for tb in range(NB):
                    b = ti % 2
                    if ti >= 2:
                        R.wait("tensor", s_dve, vt_evac[ti - 2])
                    vt_done[ti] = R.op("tensor", lambda e, b=b, oc=oc, tb=tb: e.transpose(
                        pT[b][:, 0:128], af_sb[:, oc * T + tb * 128: oc * T + (tb + 1) * 128],
                        ident[:, :]), inc=s_pe)
                    R.wait("vector", s_pe, vt_done[ti])
                    vt_evac[ti] = R.op("vector", lambda e, b=b, oc=oc, tb=tb: e.tensor_copy(
                        v_cur[:, tb * D + oc * 128: tb * D + (oc + 1) * 128], pT[b][:, 0:128]),
                        inc=s_dve)
                    ti += 1
            vt_all = vt_evac[ti - 1]
            if n_phases <= 3:
                dbg_stores()
                continue

            SCB = (4, 5, 3)
            AVB = (2, 0)
            peS, peT_, peP = {}, {}, {}
            dveM, dveE = {}, {}
            actE, actA, actTS = {}, {}, {}
            q_all = q_ready[DC - 1][1]
            k_all = k_ready[DC - 1][1]

            def hb_params(j):
                h, i = j // NB, j % NB
                oc, po = h // HPC, (h % HPC) * DK
                return i, h, oc, po

            def emit_SC(j):
                i, h, oc, po = hb_params(j)
                b = SCB[j % 3]
                if j >= 3:
                    R.wait("tensor", s_act, actE[j - 3])
                else:
                    bank_wait("tensor", b)
                if j == 0:
                    R.wait("tensor", s_act, q_all)
                    R.wait("tensor", s_dve, k_all)
                    R.wait("tensor", s_pool, carry_pool[0])
                msel = m0 if (seq_start and i == 0) else mG
                qs = q_sb[po:po + DK, oc * T + i * 128: oc * T + (i + 1) * 128]
                if i == 0:
                    kp = k_prev[po:po + DK, oc * 128:(oc + 1) * 128]
                    kc = k_cur[po:po + DK, oc * T: oc * T + 128]
                    R.op("tensor", lambda e, b=b, qs=qs, kp=kp: e.matmul(
                        pF[b][:, 0:128], qs, kp, start=True, stop=False))
                    R.op("tensor", lambda e, b=b, qs=qs, kc=kc: e.matmul(
                        pF[b][:, 128:256], qs, kc, start=False, stop=False))
                else:
                    kw = k_cur[po:po + DK, oc * T + (i - 1) * 128: oc * T + (i + 1) * 128]
                    R.op("tensor", lambda e, b=b, qs=qs, kw=kw: e.matmul(
                        pF[b][:, 0:256], qs, kw, start=True, stop=False))
                peS[j] = R.op("tensor", lambda e, b=b, msel=msel: e.matmul(
                    pF[b][:, 0:256], ident[:, :], msel[:, :], start=False, stop=True),
                    inc=s_pe)

            def emit_TR(j):
                b = j % 2
                R.wait("tensor", s_act, actTS[j])
                if j >= 2:
                    R.wait("tensor", s_dve, dveE[j - 2])
                R.op("tensor", lambda e, b=b: e.transpose(
                    pT[b][:, 0:128], pb_sb[b][:, 0:128], ident[:, :]))
                peT_[j] = R.op("tensor", lambda e, b=b: e.transpose(
                    pT[b][:, 128:256], pb_sb[b][:, 128:256], ident[:, :]), inc=s_pe)

            def emit_PV(j):
                i, h, oc, po = hb_params(j)
                ab = AVB[j % 2]
                R.wait("tensor", s_dve, dveE[j])
                if j == 0:
                    R.wait("tensor", s_dve, vt_all)
                    R.wait("tensor", s_pool, carry_pool[1])
                if j >= 2:
                    R.wait("tensor", s_act, actA[j - 2])
                else:
                    bank_wait("tensor", ab)
                v0 = (v_prev[:, h * DK: h * DK + DK] if i == 0
                      else v_cur[:, (i - 1) * D + h * DK: (i - 1) * D + h * DK + DK])
                v1 = v_cur[:, i * D + h * DK: i * D + h * DK + DK]
                R.op("tensor", lambda e, ab=ab, v0=v0, b=j % 2: e.matmul(
                    pF[ab][0:DK, 0:128], v0, ptT[b][:, 0:128], start=True, stop=False))
                peP[j] = R.op("tensor", lambda e, ab=ab, v1=v1, b=j % 2: e.matmul(
                    pF[ab][0:DK, 0:128], v1, ptT[b][:, 128:256], start=False, stop=True),
                    inc=s_pe)

            def emit_MAX(j):
                b = j % 3
                R.wait("vector", s_pe, peS[j])
                if j >= 3:
                    R.wait("vector", s_act, actE[j - 3])
                dveM[j] = R.op("vector", lambda e, b=b: e.reduce_max(
                    nmx[b][:, :], pF[SCB[b]][:, 0:256], axis=AX.X, negate=True), inc=s_dve)

            dveR = {}

            def emit_RECIP(j):
                b = j % 3
                R.wait("vector", s_act, actE[j])
                if j >= 3:
                    R.wait("vector", s_act, actTS[j - 3])
                dveR[j] = R.op("vector", lambda e, b=b: e.reciprocal(
                    rsr[b][:, :], smr[b][:, :]), inc=s_dve)

            def emit_TSc(j):
                b = j % 3
                pb = j % 2
                R.wait("scalar", s_dve, dveR[j])
                if j >= 2:
                    R.wait("scalar", s_pe, peT_[j - 2])
                actTS[j] = R.op("scalar", lambda e, b=b, pb=pb: e.activation(
                    pb_sb[pb][:, :], p_sb[b][:, :], AF.Copy, scale=rsr[b][:, 0:1]),
                    inc=s_act)

            def emit_PTE(j):
                b = j % 2
                R.wait("vector", s_pe, peT_[j])
                if j >= 2:
                    R.wait("vector", s_pe, peP[j - 2])
                dveE[j] = R.op("vector", lambda e, b=b: e.tensor_copy(
                    ptT[b][:, :], pT[b][:, :]), inc=s_dve)

            def emit_EXP(j):
                b = j % 3
                R.wait("scalar", s_dve, dveM[j])
                if j >= 3:
                    R.wait("scalar", s_act, actTS[j - 3], force=True)
                actE[j] = R.op("scalar", lambda e, b=b: e.activation(
                    p_sb[b][:, :], pF[SCB[b]][:, 0:256], AF.Exp, bias=nmx[b][:, 0:1],
                    accum_out=smr[b][:, 0:1]), inc=s_act)

            def emit_AVE(j):
                i, h, oc, po = hb_params(j)
                ab = AVB[j % 2]
                R.wait("scalar", s_pe, peP[j])
                actA[j] = R.op("scalar", lambda e, ab=ab, oc=oc, po=po, i=i: e.activation(
                    af_sb[po:po + DK, oc * T + i * 128: oc * T + (i + 1) * 128],
                    pF[ab][0:DK, 0:128], AF.Copy), inc=s_act)

            for j in range(NHB + 6):
                if j < NHB:
                    emit_SC(j)
                if 0 <= j - 3 < NHB:
                    emit_PTE(j - 3)
                if 0 <= j - 5 < NHB:
                    emit_AVE(j - 5)
                if 0 <= j - 1 < NHB:
                    emit_RECIP(j - 1)
                if j < NHB:
                    emit_MAX(j)
                if 0 <= j - 1 < NHB:
                    emit_TSc(j - 1)
                if j < NHB:
                    emit_EXP(j)
                if 0 <= j - 2 < NHB:
                    emit_TR(j - 2)
                if 0 <= j - 4 < NHB:
                    emit_PV(j - 4)

            for k in range(3):
                mm_bank_free[SCB[(NHB - 1 - k) % 3]] = ("s_act", actE[NHB - 1 - k])
            for k in range(2):
                mm_bank_free[AVB[(NHB - 1 - k) % 2]] = ("s_act", actA[NHB - 1 - k])
            attn_all = actA[NHB - 1]

            if (tile + 1) % TPS != 0:
                R.wait("gpsimd", s_pe, peS[NHB - 1])
                kc3 = k_cur[:, :].rearrange("p (oc t) -> p oc t", t=T)
                kp3 = k_prev[:, :].rearrange("p (oc t) -> p oc t", t=128)
                carry_pool[0] = R.op("gpsimd", lambda e, kc3=kc3, kp3=kp3: e.tensor_copy(
                    kp3[:, :, :], kc3[:, :, T - 128:T]), inc=s_pool)
                R.wait("gpsimd", s_pe, peP[NHB - 1])
                carry_pool[1] = R.op("gpsimd", lambda e: e.tensor_copy(
                    v_prev[:, :], v_cur[:, (NB - 1) * D: NB * D]), inc=s_pool)
            if n_phases <= 4:
                dbg_stores()
                continue

            wo_evac = {}
            if tile == 0:
                R.wait("sync", s_cw["o"], cast_done["o"])
            for oc in range(DC):
                seq, slot, wv_ = stream_stripe(so_d[oc])
                b = oc % 2
                R.wait("tensor", wv_[0], wv_[1])
                bank_wait("tensor", b)
                if oc == 0:
                    R.wait("tensor", s_act, attn_all)
                for dc2 in range(DC):
                    pev = R.op("tensor", lambda e, b=b, slot=slot, dc2=dc2: e.matmul(
                        pF[b][:, :], slot[:, dc2 * 128:(dc2 + 1) * 128],
                        af_sb[:, dc2 * T:(dc2 + 1) * T],
                        start=(dc2 == 0), stop=(dc2 == DC - 1)), inc=s_pe)
                mark_consumed(seq, pev)
                R.wait("vector", s_pe, pev)
                wo_evac[oc] = R.op("vector", lambda e, oc=oc, b=b: e.tensor_tensor(
                    x_sb[:, oc * T:(oc + 1) * T], x_sb[:, oc * T:(oc + 1) * T],
                    pF[b][:, :], ALU.add), inc=s_dve)
                mm_bank_free[b] = ("s_dve", wo_evac[oc])
            if n_phases <= 5:
                dbg_stores()
                continue

            hid_done = {}
            silu_done = {}
            if tile == 0:
                R.wait("sync", s_cw["1"], cast_done["1"])
                R.wait("sync", s_cw["3"], cast_done["3"])
            for hc in range(HC):
                gb, lb = ((4, 5), (2, 3))[hc % 2]
                seq1, slot1, wv1 = stream_stripe(s1_d[hc])
                seq3, slot3, wv3 = stream_stripe(s3_d[hc])
                if hc == DC:
                    R.wait("vector", s_pool, carry_pool[0])
                if hc == 3 * DC:
                    R.wait("vector", s_pool, carry_pool[1])
                R.wait("tensor", wv1[0], wv1[1])
                bank_wait("tensor", gb)
                for dc2 in range(DC):
                    peg = R.op("tensor", lambda e, gb=gb, slot1=slot1, dc2=dc2: e.matmul(
                        pF[gb][:, :], slot1[:, dc2 * 128:(dc2 + 1) * 128],
                        xn_sb[:, dc2 * T:(dc2 + 1) * T],
                        start=(dc2 == 0), stop=(dc2 == DC - 1)), inc=s_pe)
                mark_consumed(seq1, peg)
                R.wait("tensor", wv3[0], wv3[1])
                bank_wait("tensor", lb)
                for dc2 in range(DC):
                    pel = R.op("tensor", lambda e, lb=lb, slot3=slot3, dc2=dc2: e.matmul(
                        pF[lb][:, :], slot3[:, dc2 * 128:(dc2 + 1) * 128],
                        xn_sb[:, dc2 * T:(dc2 + 1) * T],
                        start=(dc2 == 0), stop=(dc2 == DC - 1)), inc=s_pe)
                mark_consumed(seq3, pel)
                R.wait("scalar", s_pe, peg)
                if hc >= 2:
                    R.wait("scalar", s_dve, hid_done[hc - 2])
                silu_done[hc] = R.op("scalar", lambda e, hc=hc, gb=gb: e.activation(
                    sg[hc % 2][:, :], pF[gb][:, :], AF.Silu), inc=s_act)
                mm_bank_free[gb] = ("s_act", silu_done[hc])
                R.wait("vector", s_pe, pel)
                R.wait("vector", s_act, silu_done[hc])
                hid_done[hc] = R.op("vector", lambda e, hc=hc, lb=lb: e.tensor_tensor(
                    hidden[:, hc * T:(hc + 1) * T], sg[hc % 2][:, :], pF[lb][:, :],
                    ALU.mult), inc=s_dve)
                mm_bank_free[lb] = ("s_dve", hid_done[hc])
            if n_phases <= 6:
                dbg_stores()
                continue

            if tile == 0:
                if n_cores > 1:
                    R.wait("sync", s_cc, 16)
                else:
                    R.wait("sync", s_os[0], transform_stores_all[0])
                    R.wait("sync", s_os[1], transform_stores_all[1])
            for dc2 in range(DC):
                if w2_pe_prev[dc2 % 2] is not None:
                    R.wait("sync", s_pe, w2_pe_prev[dc2 % 2])
                wv_ = R.op("sync", lambda e, dc2=dc2: e.dma_start(
                    w2buf[dc2 % 2][:, :].rearrange("p (a b) -> p a b", b=128), s2_d[dc2]),
                    inc=s_w2b[dc2 % 2], inc_by=16)
                b = 4 + (dc2 % 2)
                R.wait("tensor", s_w2b[dc2 % 2], wv_)
                bank_wait("tensor", b)
                R.wait("tensor", s_dve, hid_done[HC - 1])
                for hc in range(HC):
                    pev = R.op("tensor", lambda e, b=b, dc2=dc2, hc=hc: e.matmul(
                        pF[b][:, :], w2buf[dc2 % 2][:, hc * 128:(hc + 1) * 128],
                        hidden[:, hc * T:(hc + 1) * T],
                        start=(hc == 0), stop=(hc == HC - 1)), inc=s_pe)
                w2_pe_prev[dc2 % 2] = pev
                R.wait("vector", s_pe, pev)
                ev = R.op("vector", lambda e, dc2=dc2, b=b: e.tensor_tensor(
                    x_sb[:, dc2 * T:(dc2 + 1) * T], x_sb[:, dc2 * T:(dc2 + 1) * T],
                    pF[b][:, :], ALU.add), inc=s_dve)
                mm_bank_free[b] = ("s_dve", ev)
                R.wait("gpsimd", s_dve, ev)
                R.op("gpsimd", lambda e, dc2=dc2, t0=t0: e.dma_start(
                    out_d[dc2][:, t0:t0 + T], x_sb[:, dc2 * T:(dc2 + 1) * T]),
                    inc=s_sc[dc2], inc_by=16)
                if n_phases >= 7 and tile + 1 < NT:
                    emit_x_load(tile + 1, dc2)
                    x_loads_pre.add(tile + 1)
                    if dc2 >= 2:
                        emit_mean_chunk(tile + 1, dc2 - 2)
                    if dc2 >= 4:
                        emit_sq_chunk(tile + 1, dc2 - 4)
            if n_phases >= 7 and tile + 1 < NT:
                for dc2 in range(max(0, DC - 2), DC):
                    emit_mean_chunk(tile + 1, dc2)
                for dc2 in range(max(0, DC - 4), DC):
                    emit_sq_chunk(tile + 1, dc2)
                emit_LN_rest(tile + 1)

        for dc2 in range(DC):
            R.wait("gpsimd", s_sc[dc2], s_sc[dc2].n)

        with nc.Block() as block:
            @block.tensor
            def _(t):
                for f in R.prog["tensor"]:
                    f(t)

            @block.vector
            def _(v):
                for f in R.prog["vector"]:
                    f(v)

            @block.scalar
            def _(s):
                for f in R.prog["scalar"]:
                    f(s)

            @block.sync
            def _(sy):
                for f in R.prog["sync"]:
                    f(sy)

            @block.gpsimd
            def _(g):
                for f in R.prog["gpsimd"]:
                    f(g)

    return nc


_CACHED = {}


def _get_program():
    if "nc" not in _CACHED:
        _CACHED["nc"] = build_program()
    return _CACHED["nc"]


def kernel(L0_attn_x, L0_attn_Wq, L0_attn_Wk, L0_attn_Wv, L0_attn_Wo,
           L0_ffn_W1, L0_ffn_W2, L0_ffn_W3):
    B, S, D = L0_attn_x.shape
    HF = L0_ffn_W1.shape[0]
    DC, HC = D // 128, HF // 128
    n_cores = 8
    bpc = B // n_cores

    def lhsT_layout(W, n_out):
        return np.ascontiguousarray(
            np.asarray(W).reshape(n_out, 128, DC, 128).transpose(0, 3, 2, 1))

    wq = lhsT_layout(L0_attn_Wq, DC)
    wk = lhsT_layout(L0_attn_Wk, DC)
    wv = lhsT_layout(L0_attn_Wv, DC)
    wo = lhsT_layout(L0_attn_Wo, DC)
    w1 = lhsT_layout(L0_ffn_W1, HC)
    w3 = lhsT_layout(L0_ffn_W3, HC)
    w2 = np.ascontiguousarray(
        np.asarray(L0_ffn_W2).reshape(DC, 128, HC, 128).transpose(0, 3, 2, 1))

    x = np.asarray(L0_attn_x)
    in_maps = []
    for core in range(n_cores):
        xc = np.ascontiguousarray(
            x[core * bpc:(core + 1) * bpc].transpose(2, 0, 1).reshape(D, bpc * S)
            .reshape(DC, 128, bpc * S))
        in_maps.append({"x": xc, "wq": wq, "wk": wk, "wv": wv, "wo": wo,
                        "w1": w1, "w3": w3, "w2": w2})

    nc = _get_program()
    res = run_bass_kernel_spmd(nc, in_maps, core_ids=list(range(n_cores)))

    out = np.empty((B, S, D), dtype=np.float32)
    for core in range(n_cores):
        oc_ = np.asarray(res.results[core]["out"]).reshape(D, bpc, S)
        out[core * bpc:(core + 1) * bpc] = oc_.transpose(1, 2, 0)
    return out


if __name__ == "__main__":
    import json
    rng = np.random.default_rng(0)
    print("building program...")
    nc = _get_program()
    print("n_instructions:", nc.n_instructions())
